# version 01
# baseline (speedup 1.0000x reference)
import numpy as np
import ml_dtypes
import concourse.bass as bass
import concourse.mybir as mybir
from concourse.bass_utils import run_bass_kernel_spmd
from concourse.alu_op_type import AluOpType as ALU
from contextlib import ExitStack

F32 = mybir.dt.float32
BF16 = mybir.dt.bfloat16
AF = mybir.ActivationFunctionType
AX = mybir.AxisListType

P = 128
T = 2048
D = 1024
FF = 4096
NH = 16
DC = D // P
FC = FF // P
TW = 256
NT = T // TW
TW2 = 512
NT2 = T // TW2
ARENA_W = 51200

_DEV_STOP = None
_DEV_DBG = None


class _Op:
    __slots__ = ("ieng", "sem", "fn", "deps", "tick", "need_inc", "dbg", "is_dma",
                 "is_barrier")

    def __init__(self, ieng, sem, fn, dbg="", is_dma=False, is_barrier=False):
        self.ieng = ieng
        self.sem = sem
        self.fn = fn
        self.deps = []
        self.tick = None
        self.need_inc = False
        self.dbg = dbg
        self.is_dma = is_dma
        self.is_barrier = is_barrier


class Sched:
    IENG = ("pe", "dve", "act", "pool", "sync")
    SEMS = ("pe", "dve", "act", "pool", "sdma", "adma")

    def __init__(self):
        self.by_eng = {e: [] for e in self.IENG}
        self.last_w = {}
        self.readers = {}
        self.last_real = {}

    def _add(self, o, r, w):
        deps = []
        for b in r:
            lw = self.last_w.get(id(b))
            if lw is not None and not (lw.ieng == o.ieng == "pe"):
                deps.append(lw)
        for b in w:
            lw = self.last_w.get(id(b))
            dma = o.sem in ("sdma", "adma")
            if lw is not None and (lw.ieng != o.ieng or dma or lw.sem in ("sdma", "adma")):
                deps.append(lw)
            for rd in self.readers.get(id(b), ()):
                if rd.ieng != o.ieng or dma or rd.sem in ("sdma", "adma"):
                    deps.append(rd)
        seen = set()
        for d in deps:
            if d is not o and id(d) not in seen:
                seen.add(id(d))
                o.deps.append(d)
                d.need_inc = True
        for b in r:
            self.readers.setdefault(id(b), []).append(o)
        for b in w:
            self.last_w[id(b)] = o
            self.readers[id(b)] = []
        self.by_eng[o.ieng].append(o)
        self.last_real[o.ieng] = o
        return o

    def op(self, eng, fn, r=(), w=(), dbg=""):
        return self._add(_Op(eng, eng, fn, dbg), r, w)

    def dma(self, q, out_ap, in_ap, r=(), w=(), dbg=""):
        ieng = "sync" if q == "sdma" else "act"
        o = _Op(ieng, q, lambda e, o_=out_ap, i_=in_ap: e.dma_start(out=o_, in_=i_),
                dbg, is_dma=True)
        return self._add(o, r, w)

    def barrier(self):
        lasts = dict(self.last_real)
        for e in self.IENG:
            deps = [op for f, op in lasts.items() if op is not None and f != e]
            if not deps:
                continue
            o = _Op(e, e, lambda eng: eng.nop(), "barrier", is_barrier=True)
            for d in deps:
                o.deps.append(d)
                d.need_inc = True
            self.by_eng[e].append(o)

    def assign(self):
        cnt = {s: 0 for s in self.SEMS}
        for e in self.IENG:
            for o in self.by_eng[e]:
                if o.is_dma:
                    o.need_inc = True
                assert not (o.is_barrier and o.need_inc), "dep on barrier nop"
                if o.need_inc:
                    cnt[o.sem] += 16 if o.is_dma else 1
                    o.tick = cnt[o.sem]

    def run_engine(self, ieng, eng_obj, sems):
        hw = {}
        for o in self.by_eng[ieng]:
            for d in o.deps:
                if hw.get(d.sem, 0) < d.tick:
                    eng_obj.wait_ge(sems[d.sem], d.tick)
                    hw[d.sem] = d.tick
            inst = o.fn(eng_obj)
            if o.need_inc:
                assert inst is not None, ("inc on None inst", o.dbg)
                inst.then_inc(sems[o.sem], 16 if o.is_dma else 1)


def _cols(vec):
    return np.ascontiguousarray(np.asarray(vec, np.float32).reshape(-1, P).T)


def _host_constants():
    c = {}
    t = np.arange(T, dtype=np.float32)
    idx = np.linspace(0.0, 1.0, 64).astype(np.float32)
    freqs = (10.0 ** idx)[None, :].astype(np.float32) * np.arange(NH, dtype=np.float32)[:, None]
    angles = (t[:, None, None] * freqs[None]).astype(np.float32)
    c["cos_fm"] = np.ascontiguousarray(np.cos(angles).astype(np.float32).reshape(T, D).T)
    c["iota_pl"] = np.ascontiguousarray(np.broadcast_to(t, (P, T)))
    c["ones_col"] = np.ones((P, 1), np.float32)
    c["ones_row"] = np.ones((1, P), np.float32)
    c["ident"] = np.eye(P, dtype=np.float32)
    Es = np.zeros((DC, P, NH), np.float32)
    for ch in range(DC):
        Es[ch, 0:64, 2 * ch] = 1.0
        Es[ch, 64:128, 2 * ch + 1] = 1.0
    c["esum"] = np.ascontiguousarray(Es.transpose(1, 0, 2).reshape(P, DC * NH))
    Ee = np.zeros((NH, DC, P), np.float32)
    for ch in range(DC):
        Ee[2 * ch, ch, 0:64] = 1.0
        Ee[2 * ch + 1, ch, 64:128] = 1.0
    c["e16"] = np.ascontiguousarray(Ee.reshape(NH, DC * P))
    m = (np.arange(P)[:, None] <= np.arange(P)[None, :]).astype(np.float32)
    c["mask4"] = np.ascontiguousarray(np.tile(m, (1, 4)))
    return c


def _prep_shared(inp):
    g = {}
    A = lambda k: np.asarray(inp[k], np.float32)
    tr = lambda a: np.ascontiguousarray(np.asarray(a, np.float32).T)
    g["qk_wT"] = tr(A("L0_attn_a_qk_w"))
    g["v_wT"] = tr(A("L0_attn_a_v_w"))
    g["g_wT"] = tr(A("L0_attn_a_g_w"))
    g["o_wT"] = tr(A("L0_attn_a_o_w"))
    g["gate_wT"] = tr(A("L0_attn_a_gate_w"))
    g["up_wT"] = tr(A("L0_attn_a_up_w"))
    g["down_wT"] = tr(A("L0_attn_a_down_w"))
    g["ln1_w_c"] = _cols(A("L0_attn_a_ln1_w"))
    g["ln1_b_c"] = _cols(A("L0_attn_a_ln1_b"))
    g["conv_w_c"] = np.ascontiguousarray(
        A("L0_attn_a_conv_w").reshape(D, 4).reshape(DC, P, 4).transpose(1, 0, 2).reshape(P, DC * 4))
    g["conv_b_c"] = _cols(A("L0_attn_a_conv_b"))
    g["g_b_c"] = _cols(A("L0_attn_a_g_b"))
    g["onorm_w_c"] = _cols(A("L0_attn_a_onorm_w"))
    g["onorm_b_c"] = _cols(A("L0_attn_a_onorm_b"))
    g["gamma1_c"] = _cols(A("L0_attn_a_gamma1"))
    g["gamma2_c"] = _cols(A("L0_attn_a_gamma2"))
    g["ln2_w_c"] = _cols(A("L0_attn_a_ln2_w"))
    g["ln2_b_c"] = _cols(A("L0_attn_a_ln2_b"))
    g["gamma_attn"] = np.ascontiguousarray(A("L0_attn_a_gamma_attn").reshape(NH, 1))
    for nm in ("Wq", "Wk", "Wv", "Wo"):
        g[nm + "T"] = tr(A("L0_attn_b_" + nm))
    g["bq_c"] = _cols(A("L0_attn_b_bq"))
    g["bk_c"] = _cols(A("L0_attn_b_bk"))
    g["bo_c"] = _cols(A("L0_attn_b_bo"))
    g["bv_row"] = np.ascontiguousarray(A("L0_attn_b_bv").reshape(1, D))
    g["bk_row"] = np.ascontiguousarray(A("L0_attn_b_bk").reshape(1, D))
    g["WexpT"] = tr(A("L0_ffn_W_exp"))
    g["Wcon"] = np.ascontiguousarray(A("L0_ffn_W_con"))
    g["b_exp_c"] = _cols(A("L0_ffn_b_exp"))
    g["up_bias_c"] = _cols(A("L0_ffn_up_bias"))
    g["b_con_c"] = _cols(A("L0_ffn_b_con"))
    g["down_bias_c"] = _cols(A("L0_ffn_down_bias"))
    for nm, key in (("bup", "L0_ffn_up_weights"), ("bdn", "L0_ffn_down_weights")):
        w = A(key)
        rows = np.stack([w[:, :, 0, 0], w[:, :, 0, 1], w[:, :, 1, 0], w[:, :, 1, 1]], 1)
        g[nm + "_rows"] = np.ascontiguousarray(rows.reshape(12, 4 * 2048))
        colsf = rows.reshape(12, 4, NH, P).transpose(0, 1, 3, 2)
        g[nm + "_cols"] = np.ascontiguousarray(
            colsf.reshape(12 * 4, P, NH).transpose(1, 0, 2).reshape(P, 12 * 4 * NH))
    g.update(_host_constants())
    return g


_IN_SHAPES = {
    "qk_wT": [D, D], "v_wT": [D, D], "g_wT": [D, D], "o_wT": [D, D],
    "gate_wT": [D, FF], "up_wT": [D, FF], "down_wT": [FF, D],
    "WqT": [D, D], "WkT": [D, D], "WvT": [D, D], "WoT": [D, D],
    "WexpT": [FF, D], "Wcon": [FF, D],
    "cos_fm": [D, T], "iota_pl": [P, T],
    "ln1_w_c": [P, DC], "ln1_b_c": [P, DC], "conv_w_c": [P, DC * 4],
    "conv_b_c": [P, DC], "g_b_c": [P, DC], "onorm_w_c": [P, DC],
    "onorm_b_c": [P, DC], "gamma1_c": [P, DC], "gamma2_c": [P, DC],
    "ln2_w_c": [P, DC], "ln2_b_c": [P, DC], "gamma_attn": [NH, 1],
    "bq_c": [P, DC], "bk_c": [P, DC], "bo_c": [P, DC],
    "bv_row": [1, D], "bk_row": [1, D],
    "b_exp_c": [P, FC], "up_bias_c": [P, FC], "b_con_c": [P, DC],
    "down_bias_c": [P, DC],
    "bup_rows": [12, 4 * 2048], "bdn_rows": [12, 4 * 2048],
    "bup_cols": [P, 12 * 4 * NH], "bdn_cols": [P, 12 * 4 * NH],
    "ones_col": [P, 1], "ones_row": [1, P], "ident": [P, P],
    "esum": [P, DC * NH], "e16": [NH, DC * P], "mask4": [P, 4 * P],
}


class Reg:
    __slots__ = ("ar", "off", "w", "tok", "name")

    def __init__(self, ar, off, w, name):
        self.ar = ar
        self.off = off
        self.w = w
        self.tok = object()
        self.name = name

    @property
    def f(self):
        return self.ar.t[:, self.off:self.off + self.w]

    @property
    def b(self):
        return self.ar.t[:, self.off:self.off + self.w].bitcast(BF16)

    def fr(self, r0, r1, c0, c1):
        return self.ar.t[r0:r1, self.off + c0:self.off + c1]


class Arena:
    def __init__(self, nc, stack, width):
        self.th = stack.enter_context(nc.sbuf_tensor("ARENA", [P, width], F32))
        self.t = self.th[:]
        self.tb = self.th[:].bitcast(BF16)
        self.width = width
        self.hi = 0
        self.frees = []

    def alloc(self, name, wf32):
        wf32 = (wf32 + 3) & ~3
        for i, (o, w) in enumerate(self.frees):
            if w >= wf32:
                self.frees.pop(i)
                if w > wf32:
                    self.frees.append((o + wf32, w - wf32))
                return Reg(self, o, wf32, name)
        o = self.hi
        self.hi += wf32
        assert self.hi <= self.width, f"arena overflow at {name}: {self.hi}"
        return Reg(self, o, wf32, name)

    def free(self, *regs):
        for r in regs:
            self.frees.append((r.off, r.w))
        self.frees.sort()
        out = []
        for o, w in self.frees:
            if out and out[-1][0] + out[-1][1] == o:
                out[-1] = (out[-1][0], out[-1][1] + w)
            else:
                out.append((o, w))
        if out and out[-1][0] + out[-1][1] == self.hi:
            self.hi = out.pop()[0]
        self.frees = [tuple(x) for x in out]


class Prog:
    def __init__(self, stop=None, dbg=None):
        self.stop = stop
        self.dbg_name = dbg
        self.nc = bass.Bass("TRN2", target_bir_lowering=False)
        self.S = Sched()
        self.stack = ExitStack()

    def build(self):
        nc, S = self.nc, self.S
        hb = {}
        hb["x_fm"] = nc.dram_tensor("x_fm", [D, T], F32, kind="ExternalInput").ap()
        for nm, sh in _IN_SHAPES.items():
            hb[nm] = nc.dram_tensor(nm, sh, F32, kind="ExternalInput").ap()
        hb["y"] = nc.dram_tensor("y", [D, T], F32, kind="ExternalOutput").ap()
        hb["xmid"] = nc.dram_tensor("s_xmid", [D, T], F32, kind="ExternalOutput").ap()
        hb["x1"] = nc.dram_tensor("s_x1", [D, T], F32, kind="ExternalOutput").ap()
        hb["x2"] = nc.dram_tensor("s_x2", [D, T], F32, kind="ExternalOutput").ap()
        hb["x3"] = nc.dram_tensor("s_x3", [D, T], F32, kind="ExternalOutput").ap()
        hb["W1"] = nc.dram_tensor("s_W1", [D, FF], BF16, kind="ExternalOutput").ap()
        hb["W2"] = nc.dram_tensor("s_W2", [FF, D], BF16, kind="ExternalOutput").ap()
        hb["hbuf"] = nc.dram_tensor("s_hbuf", [P, FC * T], BF16, kind="ExternalOutput").ap()
        self.hb = hb
        self.htok = {k: object() for k in ("xmid", "x1", "x2", "x3", "y",
                                           "W1", "W2", "hbuf", "brow")}

        self.A = Arena(nc, self.stack, ARENA_W)
        self.pbs = [self.stack.enter_context(nc.psum_tensor(f"pb{i}", [P, 512], F32))
                    for i in range(7)]
        self.pb = [t[:] for t in self.pbs]
        self.pbt = [object() for _ in range(7)]
        self.pbhs = self.stack.enter_context(nc.psum_tensor("pbh", [P, 1024], BF16))
        self.pbh = self.pbhs[:]
        self.pbht = object()

        self.load_consts()
        S.barrier()
        st = self.stop
        self.p0_compose()
        S.barrier()
        self.p1_mixer()
        S.barrier()
        if st != "p1":
            self.p2_mlp()
            S.barrier()
        if st not in ("p1", "p2"):
            self.p3_ffn("x1", "x2", "f1")
            S.barrier()
        if st not in ("p1", "p2", "p3"):
            self.p4_attnb()
            S.barrier()
        if st not in ("p1", "p2", "p3", "p4"):
            self.p3_ffn("x3", "y", "f2")
            S.barrier()
        self.finish()

    def load_consts(self):
        S, hb, A = self.S, self.hb, self.A
        cc = {}
        self.cc = cc

        def new(name, w):
            cc[name] = A.alloc("c_" + name, w)
            return cc[name]

        def ld(name, src, w, used=None):
            r = new(name, w)
            S.dma("sdma", r.f[:, 0:(used or w)], src, w=[r.tok], dbg="ld_" + name)
            return r

        for nm, w in (("ln1_w", DC), ("ln1_b", DC), ("conv_b", DC), ("g_b", DC),
                      ("onorm_w", DC), ("onorm_b", DC), ("gamma1", DC),
                      ("gamma2", DC), ("ln2_w", DC), ("ln2_b", DC),
                      ("bq", DC), ("bk", DC), ("bo", DC), ("b_con", DC),
                      ("down_bias", DC)):
            ld(nm, hb[nm + "_c"], w)
        ld("conv_w", hb["conv_w_c"], DC * 4)
        ld("b_exp", hb["b_exp_c"], FC)
        ld("up_bias", hb["up_bias_c"], FC)
        ld("esum", hb["esum"], DC * NH)
        ld("mask4", hb["mask4"], 4 * P)
        ld("ident", hb["ident"], P)
        ld("iota", hb["iota_pl"], T)
        ld("ones_col", hb["ones_col"], 4, used=1)
        ld("bup_cols", hb["bup_cols"], 12 * 4 * NH)
        ld("bdn_cols", hb["bdn_cols"], 12 * 4 * NH)
        oro = new("ones_row", 128)
        S.dma("sdma", oro.fr(0, 1, 0, P), hb["ones_row"], w=[oro.tok], dbg="ld_ones_row")
        e16 = new("e16", DC * P)
        S.dma("sdma", e16.fr(0, NH, 0, DC * P), hb["e16"], w=[e16.tok], dbg="ld_e16")
        gam = new("gam", 4)
        S.dma("sdma", gam.fr(0, NH, 0, 1), hb["gamma_attn"], w=[gam.tok], dbg="ld_gam")
        idb = new("identb", P // 2)
        self.rows = new("rows", 2048)
        cwf = new("conv_wf", DC * 4)
        cbf = new("conv_bf", DC)
        ecor = new("edge", DC * 3 + 4)
        g1p = new("g1p", DC)
        C1 = new("C1", DC)
        bias2 = new("bias2", DC)
        lng = new("lng", DC)
        csum = new("csum", DC)
        bias1 = new("bias1", FC)

        hlp = new("hlp", 4)
        S.op("dve", lambda v: v.memset(hlp.f[:, 0:1], 1e-5), w=[hlp.tok], dbg="eps")
        S.op("dve", lambda v: v.memset(hlp.f[:, 1:2], -1.0), w=[hlp.tok], dbg="neg1")
        cw = cc["conv_w"]
        S.op("dve", lambda v: v.tensor_copy(out=idb.b[:, 0:P], in_=cc["ident"].f),
             r=[cc["ident"].tok], w=[idb.tok], dbg="identb")
        for c in range(DC):
            for t_ in range(3):
                S.op("dve", lambda v, c=c, t_=t_: v.tensor_reduce(
                    out=ecor.f[:, c * 3 + t_:c * 3 + t_ + 1],
                    in_=cw.f[:, 4 * c:4 * c + 3 - t_], axis=AX.X, op=ALU.add),
                    r=[cw.tok], w=[ecor.tok], dbg="edges")
        for c in range(DC):
            S.op("dve", lambda v, c=c: v.tensor_scalar(
                out=ecor.f[:, c * 3:c * 3 + 3], in0=ecor.f[:, c * 3:c * 3 + 3],
                scalar1=cc["ln1_b"].f[:, c:c + 1], scalar2=None, op0=ALU.mult),
                r=[ecor.tok, cc["ln1_b"].tok], w=[ecor.tok], dbg="edgemul")
        for c in range(DC):
            S.op("dve", lambda v, c=c: v.tensor_reduce(
                out=csum.f[:, c:c + 1], in_=cw.f[:, 4 * c:4 * c + 4], axis=AX.X, op=ALU.add),
                r=[cw.tok], w=[csum.tok], dbg="csum")
        for c in range(DC):
            S.op("dve", lambda v, c=c: v.scalar_tensor_tensor(
                out=cbf.f[:, c:c + 1], in0=csum.f[:, c:c + 1],
                scalar=cc["ln1_b"].f[:, c:c + 1], in1=cc["conv_b"].f[:, c:c + 1],
                op0=ALU.mult, op1=ALU.add),
                r=[csum.tok, cc["ln1_b"].tok, cc["conv_b"].tok], w=[cbf.tok], dbg="cbf")
        for c in range(DC):
            S.op("dve", lambda v, c=c: v.tensor_scalar(
                out=cwf.f[:, 4 * c:4 * c + 4], in0=cw.f[:, 4 * c:4 * c + 4],
                scalar1=cc["ln1_w"].f[:, c:c + 1], scalar2=None, op0=ALU.mult),
                r=[cw.tok, cc["ln1_w"].tok], w=[cwf.tok], dbg="cwf")
        S.op("dve", lambda v: v.tensor_tensor(out=C1.f, in0=cc["gamma1"].f,
             in1=cc["onorm_b"].f, op=ALU.mult),
             r=[cc["gamma1"].tok, cc["onorm_b"].tok], w=[C1.tok], dbg="C1")
        S.op("dve", lambda v: v.tensor_tensor(out=g1p.f, in0=cc["gamma1"].f,
             in1=cc["onorm_w"].f, op=ALU.mult),
             r=[cc["gamma1"].tok, cc["onorm_w"].tok], w=[g1p.tok], dbg="g1p")
        S.op("dve", lambda v: v.tensor_tensor(out=bias2.f, in0=cc["b_con"].f,
             in1=cc["down_bias"].f, op=ALU.add),
             r=[cc["b_con"].tok, cc["down_bias"].tok], w=[bias2.tok], dbg="bias2")
        S.op("act", lambda s: s.activation(out=gam.fr(0, NH, 0, 1), in_=gam.fr(0, NH, 0, 1),
             func=AF.Ln), r=[gam.tok], w=[gam.tok], dbg="lngam")
        for c in range(DC):
            S.op("pe", lambda t, c=c: t.matmul(self.pb[6][:, 0:1],
                 e16.fr(0, NH, c * P, (c + 1) * P), gam.fr(0, NH, 0, 1),
                 start=True, stop=True), r=[e16.tok, gam.tok], w=[self.pbt[6]], dbg="lngx")
            S.op("dve", lambda v, c=c: v.tensor_copy(out=lng.f[:, c:c + 1],
                 in_=self.pb[6][:, 0:1]), r=[self.pbt[6]], w=[lng.tok], dbg="lngc")

    def bcast(self, row_ap, bank, w, extra_r, dbg):
        oro = self.cc["ones_row"]
        self.S.op("pe", lambda t, r_=row_ap, b=bank, w_=w: t.matmul(
            self.pb[b][:, 0:w_], oro.fr(0, 1, 0, P), r_, start=True, stop=True),
            r=[oro.tok] + list(extra_r), w=[self.pbt[bank]], dbg=dbg)

    def ln_stats(self, xt_ap, xtok, sqr, w, pref, sq_chunked=None):
        S, R = self.S, self.rows
        oc_ = self.cc["ones_col"]
        for c in range(DC):
            S.op("pe", lambda t, c=c: t.matmul(self.pb[0][0:1, 0:w], oc_.f[:, 0:1],
                 xt_ap[:, c * w:(c + 1) * w], start=(c == 0), stop=(c == DC - 1)),
                 r=[xtok, oc_.tok], w=[self.pbt[0]], dbg=pref + "sum")
        for c in range(DC):
            S.op("act", lambda s, c=c: s.activation(out=sqr.f[:, 0:w],
                 in_=xt_ap[:, c * w:(c + 1) * w], func=AF.Square),
                 r=[xtok], w=[sqr.tok], dbg=pref + "sq")
            S.op("pe", lambda t, c=c: t.matmul(self.pb[1][0:1, 0:w], oc_.f[:, 0:1],
                 sqr.f[:, 0:w], start=(c == 0), stop=(c == DC - 1)),
                 r=[sqr.tok, oc_.tok], w=[self.pbt[1]], dbg=pref + "sumsq")
        rt = R.tok
        S.op("dve", lambda v: v.tensor_scalar(out=R.fr(0, 1, 0, w), in0=self.pb[0][0:1, 0:w],
             scalar1=1.0 / D, scalar2=None, op0=ALU.mult), r=[self.pbt[0]], w=[rt], dbg=pref + "m")
        S.op("dve", lambda v: v.tensor_scalar(out=R.fr(0, 1, w, 2 * w), in0=self.pb[1][0:1, 0:w],
             scalar1=1.0 / D, scalar2=None, op0=ALU.mult), r=[self.pbt[1]], w=[rt], dbg=pref + "e2")
        S.op("dve", lambda v: v.tensor_tensor(out=R.fr(0, 1, 2 * w, 3 * w), in0=R.fr(0, 1, 0, w),
             in1=R.fr(0, 1, 0, w), op=ALU.mult), r=[rt], w=[rt], dbg=pref + "m2")
        S.op("dve", lambda v: v.tensor_tensor(out=R.fr(0, 1, 2 * w, 3 * w), in0=R.fr(0, 1, w, 2 * w),
             in1=R.fr(0, 1, 2 * w, 3 * w), op=ALU.subtract), r=[rt], w=[rt], dbg=pref + "v")
        ht = self.cc["hlp"]
        S.op("act", lambda s: s.activation(out=R.fr(0, 1, 3 * w, 4 * w), in_=R.fr(0, 1, 2 * w, 3 * w),
             func=AF.Sqrt, bias=ht.fr(0, 1, 0, 1)), r=[rt, ht.tok], w=[rt], dbg=pref + "sd")
        S.op("dve", lambda v: v.reciprocal(out=R.fr(0, 1, 3 * w, 4 * w), in_=R.fr(0, 1, 3 * w, 4 * w)),
             r=[rt], w=[rt], dbg=pref + "r")

    def ln_apply(self, xt_ap, xtok, out_ap, otok, w, pref, tmp=None, scale_col=None):
        S, R = self.S, self.rows
        rt = R.tok
        S.op("dve", lambda v: v.tensor_tensor(out=R.fr(0, 1, w, 2 * w), in0=R.fr(0, 1, 0, w),
             in1=R.fr(0, 1, 3 * w, 4 * w), op=ALU.mult), r=[rt], w=[rt], dbg=pref + "mr")
        self.bcast(R.fr(0, 1, 3 * w, 4 * w), 2, w, [rt], pref + "rpl")
        self.bcast(R.fr(0, 1, w, 2 * w), 3, w, [rt], pref + "mrpl")
        if tmp is not None:
            for c in range(DC):
                S.op("dve", lambda v, c=c: v.tensor_tensor(out=tmp.f[:, 0:w],
                     in0=xt_ap[:, c * w:(c + 1) * w], in1=self.pb[2][:, 0:w], op=ALU.mult),
                     r=[xtok, self.pbt[2]], w=[tmp.tok], dbg=pref + f"a{c}")
                if scale_col is None:
                    S.op("dve", lambda v, c=c: v.tensor_tensor(out=out_ap[:, c * w:(c + 1) * w],
                         in0=tmp.f[:, 0:w], in1=self.pb[3][:, 0:w], op=ALU.subtract),
                         r=[tmp.tok, self.pbt[3]], w=[otok], dbg=pref + f"b{c}")
                else:
                    reg, _ = scale_col
                    S.op("dve", lambda v, c=c: v.tensor_tensor(out=tmp.f[:, 0:w],
                         in0=tmp.f[:, 0:w], in1=self.pb[3][:, 0:w], op=ALU.subtract),
                         r=[tmp.tok, self.pbt[3]], w=[tmp.tok], dbg=pref + f"b{c}")
                    S.op("dve", lambda v, c=c, reg=reg: v.tensor_scalar(
                         out=out_ap[:, c * w:(c + 1) * w], in0=tmp.f[:, 0:w],
                         scalar1=reg.f[:, c:c + 1], scalar2=None, op0=ALU.mult),
                         r=[tmp.tok, reg.tok], w=[otok], dbg=pref + f"s{c}")
            return
        for c in range(DC):
            S.op("dve", lambda v, c=c: v.tensor_tensor(out=out_ap[:, c * w:(c + 1) * w],
                 in0=xt_ap[:, c * w:(c + 1) * w], in1=self.pb[2][:, 0:w], op=ALU.mult),
                 r=[xtok, self.pbt[2]], w=[otok], dbg=pref + f"a{c}")
        for c in range(DC):
            S.op("dve", lambda v, c=c: v.tensor_tensor(out=out_ap[:, c * w:(c + 1) * w],
                 in0=out_ap[:, c * w:(c + 1) * w], in1=self.pb[3][:, 0:w], op=ALU.subtract),
                 r=[otok, self.pbt[3]], w=[otok], dbg=pref + f"b{c}")

    def p0_compose(self):
        S, hb, A, cc = self.S, self.hb, self.A, self.cc
        G = A.alloc("p0_G", FF)
        planes = A.alloc("p0_pl", 4 * 2048)
        m1 = A.alloc("p0_m1", FC * TW)
        m2 = A.alloc("p0_m2", FC * TW)
        wsl = A.alloc("p0_wsl", FC * TW)
        bm = A.alloc("p0_bm", FC)
        bs = A.alloc("p0_bs", FC)
        pltok = [object() for _ in range(4)]

        def build_G(rows_hb, down):
            for b in range(FC):
                S.op("dve", lambda v, b=b: v.tensor_copy(out=G.f[:, b * P:(b + 1) * P],
                     in_=cc["ident"].f), r=[cc["ident"].tok], w=[G.tok], dbg="Ginit")
            order = range(6, -1, -1) if down else range(7)
            for j in order:
                s = 1 << j
                nblk = 64 // s
                for ci in range(4):
                    S.dma("adma", planes.f[:, ci * 2048:(ci + 1) * 2048],
                          rows_hb[j:j + 1, ci * 2048:(ci + 1) * 2048].partition_broadcast(P).squeeze(1),
                          w=[pltok[ci]], dbg=f"pl{j}_{ci}")
                Gv = G.f.rearrange("p (b n two sg) -> p b n two sg", b=FC, n=nblk, two=2)
                Ga, Gb = Gv[:, :, :, 0, :], Gv[:, :, :, 1, :]
                PL = [planes.f[:, ci * 2048:(ci + 1) * 2048].rearrange(
                    "p (b n sg) -> p b n sg", b=FC, n=nblk) for ci in range(4)]
                iA, iB = (0, 2) if down else (0, 1)
                iC, iD = (1, 3) if down else (2, 3)
                ta = m1.f[:, 0:2048].rearrange("p (b n sg) -> p b n sg", b=FC, n=nblk)
                tb = m1.f[:, 2048:4096].rearrange("p (b n sg) -> p b n sg", b=FC, n=nblk)
                tc_ = m2.f[:, 0:2048].rearrange("p (b n sg) -> p b n sg", b=FC, n=nblk)
                td = m2.f[:, 2048:4096].rearrange("p (b n sg) -> p b n sg", b=FC, n=nblk)
                S.op("dve", lambda v, Ga=Ga, pl=PL[iA], ta=ta: v.tensor_tensor(out=ta, in0=Ga, in1=pl, op=ALU.mult), r=[G.tok, pltok[iA]], w=[m1.tok], dbg=f"G{j}a")
                S.op("pool", lambda g, Gb=Gb, pl=PL[iB], tb=tb: g.tensor_tensor(out=tb, in0=Gb, in1=pl, op=ALU.mult), r=[G.tok, pltok[iB]], w=[m1.tok], dbg=f"G{j}b")
                S.op("dve", lambda v, Ga=Ga, pl=PL[iC], tc_=tc_: v.tensor_tensor(out=tc_, in0=Ga, in1=pl, op=ALU.mult), r=[G.tok, pltok[iC]], w=[m2.tok], dbg=f"G{j}c")
                S.op("pool", lambda g, Gb=Gb, pl=PL[iD], td=td: g.tensor_tensor(out=td, in0=Gb, in1=pl, op=ALU.mult), r=[G.tok, pltok[iD]], w=[m2.tok], dbg=f"G{j}d")
                S.op("dve", lambda v, Ga=Ga, ta=ta, tb=tb: v.tensor_tensor(out=Ga, in0=ta, in1=tb, op=ALU.add), r=[m1.tok], w=[G.tok], dbg=f"G{j}e")
                S.op("pool", lambda g, Gb=Gb, tc_=tc_, td=td: g.tensor_tensor(out=Gb, in0=tc_, in1=td, op=ALU.add), r=[m2.tok], w=[G.tok], dbg=f"G{j}f")

        def c_stage_coeffs(j, cch, up):
            sc = 1 << (j - 7)
            bit = (cch >> (j - 7)) & 1
            cA = cch & ~(1 << (j - 7))
            cB = cA | (1 << (j - 7))
            k0 = ((cch >> (j - 6)) << j) + ((cch & (sc - 1)) << 7)
            col = k0 >> 7
            if up:
                ia, ib = ((0, 1) if bit == 0 else (2, 3))
            else:
                ia, ib = ((0, 2) if bit == 0 else (1, 3))
            cols = cc["bup_cols" if up else "bdn_cols"]
            colA = cols.f[:, (j * 4 + ia) * NH + col:(j * 4 + ia) * NH + col + 1]
            colB = cols.f[:, (j * 4 + ib) * NH + col:(j * 4 + ib) * NH + col + 1]
            return cA, cB, colA, colB, cols.tok

        def c_stages(src, dst, w, up, pref):
            cur, nxt = src, dst
            order = range(7, 12) if up else range(11, 6, -1)
            for j in order:
                for cch in range(FC):
                    cA, cB, colA, colB, ctok = c_stage_coeffs(j, cch, up)
                    S.op("act", lambda s, cur=cur, nxt=nxt, cB=cB, colB=colB, cch=cch:
                         s.activation(out=nxt.f[:, cch * w:(cch + 1) * w],
                         in_=cur.f[:, cB * w:(cB + 1) * w], func=AF.Copy, scale=colB), r=[cur.tok, ctok], w=[nxt.tok],
                         dbg=f"{pref}cs{j}_{cch}a")
                    S.op("dve", lambda v, cur=cur, nxt=nxt, cA=cA, colA=colA, cch=cch:
                         v.scalar_tensor_tensor(out=nxt.f[:, cch * w:(cch + 1) * w],
                         in0=cur.f[:, cA * w:(cA + 1) * w], scalar=colA,
                         in1=nxt.f[:, cch * w:(cch + 1) * w], op0=ALU.mult, op1=ALU.add),
                         r=[cur.tok, ctok, nxt.tok], w=[nxt.tok], dbg=f"{pref}cs{j}_{cch}b")
                cur, nxt = nxt, cur
            return cur

        build_G(hb["bup_rows"], False)
        for sl in range(4):
            S.dma("sdma", wsl.f.rearrange("p (b t) -> p b t", b=FC),
                  hb["WexpT"].rearrange("(b p) i -> p b i", p=P)[:, :, sl * TW:(sl + 1) * TW],
                  w=[wsl.tok], dbg=f"wexp{sl}")
            for b in range(FC):
                S.op("pe", lambda t, b=b: t.matmul(self.pb[5][:, 0:TW],
                     G.f[:, b * P:(b + 1) * P], wsl.f[:, b * TW:(b + 1) * TW],
                     start=True, stop=True), r=[G.tok, wsl.tok], w=[self.pbt[5]], dbg=f"gm{sl}_{b}")
                S.op("dve", lambda v, b=b: v.tensor_copy(out=m1.f[:, b * TW:(b + 1) * TW],
                     in_=self.pb[5][:, 0:TW]), r=[self.pbt[5]], w=[m1.tok], dbg=f"ge{sl}_{b}")
            fin = c_stages(m1, m2, TW, True, f"u{sl}")
            S.op("dve", lambda v, fin=fin: v.tensor_copy(out=wsl.b[:, 0:FC * TW], in_=fin.f),
                 r=[fin.tok], w=[wsl.tok], dbg=f"cst{sl}")
            oth = m2 if fin is m1 else m1
            for b in range(FC):
                for gg in range(2):
                    S.op("pe", lambda t, b=b, gg=gg: t.transpose(self.pbh[:, 0:P],
                         wsl.b[:, b * TW + gg * P:b * TW + (gg + 1) * P],
                         cc["identb"].b[:, 0:P]), r=[wsl.tok, cc["identb"].tok],
                         w=[self.pbht], dbg=f"tr{sl}_{b}_{gg}")
                    S.op("dve", lambda v, b=b, gg=gg, oth=oth: v.tensor_copy(
                         out=oth.b[:, (gg * FC + b) * P:(gg * FC + b + 1) * P],
                         in_=self.pbh[:, 0:P]), r=[self.pbht], w=[oth.tok], dbg=f"tc{sl}_{b}_{gg}")
            for gg in range(2):
                S.dma("adma", hb["W1"][sl * TW + gg * P:sl * TW + (gg + 1) * P, :],
                      oth.b[:, gg * FC * P:(gg + 1) * FC * P], r=[oth.tok],
                      w=[self.htok["W1"]], dbg=f"w1o{sl}_{gg}")
        for b in range(FC):
            S.op("pe", lambda t, b=b: t.matmul(self.pb[5][:, 0:1], G.f[:, b * P:(b + 1) * P],
                 cc["b_exp"].f[:, b:b + 1], start=True, stop=True),
                 r=[G.tok, cc["b_exp"].tok], w=[self.pbt[5]], dbg=f"b1m{b}")
            S.op("dve", lambda v, b=b: v.tensor_copy(out=bm.f[:, b:b + 1],
                 in_=self.pb[5][:, 0:1]), r=[self.pbt[5]], w=[bm.tok], dbg=f"b1e{b}")
        cur, nxt = bm, bs
        for j in range(7, 12):
            for cch in range(FC):
                cA, cB, colA, colB, ctok = c_stage_coeffs(j, cch, True)
                S.op("act", lambda s, cur=cur, nxt=nxt, cB=cB, colB=colB, cch=cch:
                     s.activation(out=nxt.f[:, cch:cch + 1], in_=cur.f[:, cB:cB + 1],
                     func=AF.Copy, scale=colB), r=[cur.tok, ctok],
                     w=[nxt.tok], dbg=f"b1c{j}_{cch}a")
                S.op("dve", lambda v, cur=cur, nxt=nxt, cA=cA, colA=colA, cch=cch:
                     v.scalar_tensor_tensor(out=nxt.f[:, cch:cch + 1],
                     in0=cur.f[:, cA:cA + 1], scalar=colA, in1=nxt.f[:, cch:cch + 1],
                     op0=ALU.mult, op1=ALU.add), r=[cur.tok, ctok, nxt.tok],
                     w=[nxt.tok], dbg=f"b1c{j}_{cch}b")
            cur, nxt = nxt, cur
        S.op("dve", lambda v, cur=cur: v.tensor_tensor(out=cc["bias1"].f, in0=cur.f[:, 0:FC],
             in1=cc["up_bias"].f, op=ALU.add), r=[cur.tok, cc["up_bias"].tok],
             w=[cc["bias1"].tok], dbg="bias1")

        build_G(hb["bdn_rows"], True)
        for sl in range(4):
            S.dma("sdma", wsl.f.rearrange("p (b t) -> p b t", b=FC),
                  hb["Wcon"].rearrange("(b p) i -> p b i", p=P)[:, :, sl * TW:(sl + 1) * TW],
                  w=[wsl.tok], dbg=f"wcon{sl}")
            fin = c_stages(wsl, m1, TW, False, f"d{sl}")
            for b in range(FC):
                S.op("pe", lambda t, b=b, fin=fin: t.matmul(self.pb[5][:, 0:TW],
                     G.f[:, b * P:(b + 1) * P], fin.f[:, b * TW:(b + 1) * TW],
                     start=True, stop=True), r=[G.tok, fin.tok], w=[self.pbt[5]], dbg=f"g2m{sl}_{b}")
                S.op("dve", lambda v, b=b: v.tensor_copy(out=m2.b[:, b * TW:(b + 1) * TW],
                     in_=self.pb[5][:, 0:TW]), r=[self.pbt[5]], w=[m2.tok], dbg=f"g2e{sl}_{b}")
            S.dma("adma", hb["W2"].rearrange("(b p) i -> p b i", p=P)[:, :, sl * TW:(sl + 1) * TW],
                  m2.b[:, 0:FC * TW].rearrange("p (b t) -> p b t", b=FC), r=[m2.tok],
                  w=[self.htok["W2"]], dbg=f"w2o{sl}")
        A.free(G, planes, m1, m2, wsl, bm, bs)

    def preload_w(self, names, dsts, pref):
        S, hb, A = self.S, self.hb, self.A
        sl = A.alloc(pref + "_slab", 2 * 2048)
        stok = [object(), object()]
        i = 0
        for nm, dst in zip(names, dsts):
            for q in range(4):
                par = i % 2
                dsl = sl.f[:, par * 2048:(par + 1) * 2048]
                S.dma("sdma", dsl.rearrange("p (c o) -> p c o", c=2),
                      hb[nm].rearrange("(c p) o -> p c o", p=P)[:, 2 * q:2 * q + 2, :],
                      w=[stok[par]], dbg=f"{pref}ld{nm}{q}")
                S.op("dve", lambda v, dst=dst, q=q, dsl=dsl: v.tensor_copy(
                     out=dst.b[:, q * 2048:(q + 1) * 2048], in_=dsl),
                     r=[stok[par]], w=[dst.tok], dbg=f"{pref}cs{nm}{q}")
                i += 1
        A.free(sl)

    def p1_mixer(self):
        S, hb, A, cc = self.S, self.hb, self.A, self.cc
        wqk = A.alloc("p1_wqk", DC * D // 2)
        wv = A.alloc("p1_wv", DC * D // 2)
        wg = A.alloc("p1_wg", DC * D // 2)
        wo = A.alloc("p1_wo", DC * D // 2)
        self.preload_w(("qk_wT", "v_wT", "g_wT", "o_wT"), (wqk, wv, wg, wo), "p1")

        xin = [A.alloc(f"p1_x{i}", DC * TW) for i in range(2)]
        xh1 = A.alloc("p1_xh1", DC * TW)
        cbuf = A.alloc("p1_cb", DC * (TW + 4))
        xcb = A.alloc("p1_xcb", DC * TW // 2)
        kpad = A.alloc("p1_k", DC * (TW + 4))
        vbuf = A.alloc("p1_v", DC * TW)
        gbuf = A.alloc("p1_g", DC * TW)
        cosb = A.alloc("p1_cos", DC * TW)
        s1 = A.alloc("p1_s1", DC * TW)
        sqr = A.alloc("p1_sq", TW)
        ob = A.alloc("p1_ob", DC * TW // 2)
        xmid = [A.alloc(f"p1_xm{i}", DC * TW) for i in range(2)]
        dpl = A.alloc("p1_dpl", 2 * TW)
        carry = A.alloc("p1_carry", DC)
        R = self.rows

        cv = cbuf.f.rearrange("p (c t) -> p c t", c=DC, t=TW + 4)
        kvv = kpad.f.rearrange("p (c t) -> p c t", c=DC, t=TW + 4)

        S.op("dve", lambda v: v.memset(carry.f, 0.0), w=[carry.tok], dbg="carry0")
        S.op("dve", lambda v: v.memset(cv[:, :, 0:3], 0.0), w=[cbuf.tok], dbg="pad0")
        S.op("dve", lambda v: v.memset(kvv[:, :, 0:1], 0.0), w=[kpad.tok], dbg="kpad0")

        for ti in range(NT):
            par = ti % 2
            xt, xmt = xin[par], xmid[par]
            t0 = ti * TW
            S.dma("sdma", xt.f.rearrange("p (c t) -> p c t", c=DC),
                  hb["x_fm"].rearrange("(c p) t -> p c t", p=P)[:, :, t0:t0 + TW],
                  w=[xt.tok], dbg=f"xin{ti}")
            S.dma("adma", cosb.f.rearrange("p (c t) -> p c t", c=DC),
                  hb["cos_fm"].rearrange("(c p) t -> p c t", p=P)[:, :, t0:t0 + TW],
                  w=[cosb.tok], dbg=f"cosin{ti}")
            self.ln_stats(xt.f, xt.tok, sqr, TW, f"l{ti}")
            rt = R.tok
            S.op("dve", lambda v: v.tensor_tensor(out=R.fr(0, 1, 4 * TW, 5 * TW), in0=R.fr(0, 1, 3 * TW, 4 * TW), in1=R.fr(0, 1, 3 * TW, 4 * TW), op=ALU.mult), r=[rt], w=[rt], dbg=f"r1s{ti}")
            S.op("dve", lambda v: v.tensor_tensor(out=R.fr(0, 1, 4 * TW, 5 * TW), in0=R.fr(0, 1, 2 * TW, 3 * TW), in1=R.fr(0, 1, 4 * TW, 5 * TW), op=ALU.mult), r=[rt], w=[rt], dbg=f"v2{ti}")
            S.op("act", lambda s: s.activation(out=R.fr(0, 1, 4 * TW, 5 * TW), in_=R.fr(0, 1, 4 * TW, 5 * TW), func=AF.Sqrt, bias=cc["hlp"].fr(0, 1, 0, 1)), r=[rt, cc["hlp"].tok], w=[rt], dbg=f"sd2{ti}")
            S.op("dve", lambda v: v.reciprocal(out=R.fr(0, 1, 4 * TW, 5 * TW), in_=R.fr(0, 1, 4 * TW, 5 * TW)), r=[rt], w=[rt], dbg=f"r2{ti}")
            S.op("dve", lambda v: v.tensor_tensor(out=R.fr(0, 1, 5 * TW, 6 * TW), in0=R.fr(0, 1, 0, TW), in1=R.fr(0, 1, 3 * TW, 4 * TW), op=ALU.mult), r=[rt], w=[rt], dbg=f"B1{ti}")
            self.bcast(R.fr(0, 1, 3 * TW, 4 * TW), 2, TW, [rt], f"r1p{ti}")
            self.bcast(R.fr(0, 1, 5 * TW, 6 * TW), 3, TW, [rt], f"B1p{ti}")
            self.bcast(R.fr(0, 1, 4 * TW, 5 * TW), 4, TW, [rt], f"r2p{ti}")
            for c in range(DC):
                S.op("dve", lambda v, c=c, xt=xt: v.tensor_tensor(out=xh1.f[:, c * TW:(c + 1) * TW], in0=xt.f[:, c * TW:(c + 1) * TW], in1=self.pb[2][:, 0:TW], op=ALU.mult), r=[xt.tok, self.pbt[2]], w=[xh1.tok], dbg=f"xa{ti}_{c}")
            for c in range(DC):
                S.op("dve", lambda v, c=c: v.tensor_tensor(out=xh1.f[:, c * TW:(c + 1) * TW], in0=xh1.f[:, c * TW:(c + 1) * TW], in1=self.pb[3][:, 0:TW], op=ALU.subtract), r=[xh1.tok, self.pbt[3]], w=[xh1.tok], dbg=f"xb{ti}_{c}")
            for c in range(DC):
                S.op("dve", lambda v, c=c: v.tensor_tensor(out=cv[:, c, 3:3 + TW], in0=xh1.f[:, c * TW:(c + 1) * TW], in1=self.pb[4][:, 0:TW], op=ALU.mult), r=[xh1.tok, self.pbt[4]], w=[cbuf.tok], dbg=f"x2{ti}_{c}")
            for c in range(DC):
                base = c * (TW + 4)
                o_ = s1.f[:, c * TW:(c + 1) * TW]
                S.op("dve", lambda v, base=base, o_=o_, c=c: v.tensor_scalar(out=o_, in0=cbuf.f[:, base:base + TW], scalar1=cc["conv_wf"].f[:, 4 * c:4 * c + 1], scalar2=cc["conv_bf"].f[:, c:c + 1], op0=ALU.mult, op1=ALU.add), r=[cbuf.tok, cc["conv_wf"].tok, cc["conv_bf"].tok], w=[s1.tok], dbg=f"c0_{ti}_{c}")
                for kk in (1, 2):
                    S.op("dve", lambda v, base=base, o_=o_, c=c, kk=kk: v.scalar_tensor_tensor(out=o_, in0=cbuf.f[:, base + kk:base + kk + TW], scalar=cc["conv_wf"].f[:, 4 * c + kk:4 * c + kk + 1], in1=o_, op0=ALU.mult, op1=ALU.add), r=[cbuf.tok, cc["conv_wf"].tok, s1.tok], w=[s1.tok], dbg=f"c{kk}_{ti}_{c}")
                S.op("dve", lambda v, base=base, o_=o_, c=c: v.scalar_tensor_tensor(out=xcb.b[:, c * TW:(c + 1) * TW], in0=cbuf.f[:, base + 3:base + 3 + TW], scalar=cc["conv_wf"].f[:, 4 * c + 3:4 * c + 4], in1=o_, op0=ALU.mult, op1=ALU.add), r=[cbuf.tok, cc["conv_wf"].tok, s1.tok], w=[xcb.tok], dbg=f"c3_{ti}_{c}")
            if ti == 0:
                for c in range(DC):
                    S.op("dve", lambda v, c=c: v.tensor_tensor(out=xcb.b[:, c * TW:c * TW + 3], in0=xcb.b[:, c * TW:c * TW + 3], in1=cc["edge"].f[:, c * 3:c * 3 + 3], op=ALU.subtract), r=[xcb.tok, cc["edge"].tok], w=[xcb.tok], dbg=f"edg{c}")
            if ti < NT - 1:
                S.op("dve", lambda v: v.tensor_copy(out=cv[:, :, 0:3], in_=cv[:, :, TW:TW + 3]), r=[cbuf.tok], w=[cbuf.tok], dbg=f"pcp{ti}")
            for pi, (wres, nm) in enumerate(((wqk, "qk"), (wv, "v"), (wg, "g"))):
                bank = 5 if pi != 2 else 6
                for oc in range(DC):
                    for kc in range(DC):
                        S.op("pe", lambda t, oc=oc, kc=kc, wres=wres, bank=bank: t.matmul(
                            self.pb[bank][:, 0:TW], wres.b[:, kc * D + oc * P:kc * D + (oc + 1) * P],
                            xcb.b[:, kc * TW:(kc + 1) * TW], start=(kc == 0), stop=(kc == DC - 1)),
                            r=[xcb.tok, wres.tok], w=[self.pbt[bank]], dbg=f"{nm}m{ti}_{oc}")
                    if nm == "qk":
                        S.op("dve", lambda v, oc=oc: v.tensor_copy(out=kvv[:, oc, 1:1 + TW], in_=self.pb[5][:, 0:TW]), r=[self.pbt[5]], w=[kpad.tok], dbg=f"qke{ti}_{oc}")
                    elif nm == "v":
                        S.op("dve", lambda v, oc=oc: v.tensor_copy(out=vbuf.f[:, oc * TW:(oc + 1) * TW], in_=self.pb[5][:, 0:TW]), r=[self.pbt[5]], w=[vbuf.tok], dbg=f"ve{ti}_{oc}")
                    else:
                        S.op("act", lambda s, oc=oc: s.activation(out=gbuf.f[:, oc * TW:(oc + 1) * TW], in_=self.pb[6][:, 0:TW], func=AF.Sigmoid, bias=cc["g_b"].f[:, oc:oc + 1]), r=[self.pbt[6], cc["g_b"].tok], w=[gbuf.tok], dbg=f"ge{ti}_{oc}")
            for c in range(DC):
                S.op("act", lambda s, c=c: s.activation(out=sqr.f[:, 0:TW], in_=kvv[:, c, 1:1 + TW], func=AF.Square), r=[kpad.tok], w=[sqr.tok], dbg=f"ks{ti}_{c}")
                S.op("pe", lambda t, c=c: t.matmul(self.pb[0][0:NH, 0:TW], cc["esum"].f[:, c * NH:(c + 1) * NH], sqr.f[:, 0:TW], start=(c == 0), stop=(c == DC - 1)), r=[sqr.tok, cc["esum"].tok], w=[self.pbt[0]], dbg=f"ss{ti}_{c}")
            S.op("act", lambda s: s.activation(out=R.fr(0, NH, 6 * TW, 7 * TW), in_=self.pb[0][0:NH, 0:TW], func=AF.Sqrt), r=[self.pbt[0]], w=[R.tok], dbg=f"rn1{ti}")
            S.op("dve", lambda v: v.tensor_scalar(out=R.fr(0, NH, 6 * TW, 7 * TW), in0=R.fr(0, NH, 6 * TW, 7 * TW), scalar1=1e-12, scalar2=None, op0=ALU.max), r=[R.tok], w=[R.tok], dbg=f"rn2{ti}")
            S.op("dve", lambda v: v.reciprocal(out=R.fr(0, NH, 6 * TW, 7 * TW), in_=R.fr(0, NH, 6 * TW, 7 * TW)), r=[R.tok], w=[R.tok], dbg=f"rn3{ti}")
            for c in range(DC):
                S.op("pe", lambda t, c=c: t.matmul(self.pb[1][:, 0:TW], cc["e16"].fr(0, NH, c * P, (c + 1) * P), R.fr(0, NH, 6 * TW, 7 * TW), start=True, stop=True), r=[R.tok, cc["e16"].tok], w=[self.pbt[1]], dbg=f"rx{ti}_{c}")
                S.op("dve", lambda v, c=c: v.tensor_tensor(out=kvv[:, c, 1:1 + TW], in0=kvv[:, c, 1:1 + TW], in1=self.pb[1][:, 0:TW], op=ALU.mult), r=[kpad.tok, self.pbt[1]], w=[kpad.tok], dbg=f"kn{ti}_{c}")
            S.op("dve", lambda v: v.tensor_tensor(out=s1.f.rearrange("p (c t) -> p c t", c=DC), in0=cosb.f.rearrange("p (c t) -> p c t", c=DC), in1=kvv[:, :, 0:TW], op=ALU.add), r=[cosb.tok, kpad.tok], w=[s1.tok], dbg=f"m1{ti}")
            S.op("pool", lambda g: g.tensor_tensor(out=s1.f, in0=s1.f, in1=vbuf.f, op=ALU.mult), r=[s1.tok, vbuf.tok], w=[s1.tok], dbg=f"m2{ti}")
            S.op("dve", lambda v: v.tensor_tensor(out=s1.f, in0=s1.f, in1=gbuf.f, op=ALU.mult), r=[s1.tok, gbuf.tok], w=[s1.tok], dbg=f"m3{ti}")
            if ti < NT - 1:
                S.op("dve", lambda v: v.tensor_copy(out=kvv[:, :, 0:1], in_=kvv[:, :, TW:TW + 1]), r=[kpad.tok], w=[kpad.tok], dbg=f"kc{ti}")
            cum = vbuf
            for c in range(DC):
                slc = slice(c * TW, (c + 1) * TW)
                S.op("act", lambda s, c=c, t0=t0: s.activation(out=dpl.f[:, 0:TW], in_=cc["iota"].f[:, t0:t0 + TW], func=AF.Exp, scale=cc["lng"].f[:, c:c + 1]), r=[cc["iota"].tok, cc["lng"].tok], w=[dpl.tok], dbg=f"dk{ti}_{c}")
                S.op("dve", lambda v, slc=slc: v.tensor_tensor(out=s1.f[:, slc], in0=s1.f[:, slc], in1=dpl.f[:, 0:TW], op=ALU.mult), r=[s1.tok, dpl.tok], w=[s1.tok], dbg=f"zd{ti}_{c}")
                S.op("dve", lambda v: v.tensor_scalar(out=dpl.f[:, TW:2 * TW], in0=dpl.f[:, 0:TW], scalar1=1e-6, scalar2=None, op0=ALU.add), r=[dpl.tok], w=[dpl.tok], dbg=f"ra{ti}_{c}")
                S.op("dve", lambda v: v.reciprocal(out=dpl.f[:, TW:2 * TW], in_=dpl.f[:, TW:2 * TW]), r=[dpl.tok], w=[dpl.tok], dbg=f"rb{ti}_{c}")
                S.op("dve", lambda v, slc=slc, c=c: v.tensor_tensor_scan(out=cum.f[:, slc], data0=cc["ones_col"].f[:, 0:1].broadcast_to([P, TW]), data1=s1.f[:, slc], initial=carry.f[:, c:c + 1], op0=ALU.mult, op1=ALU.add), r=[s1.tok, carry.tok, cc["ones_col"].tok], w=[cum.tok], dbg=f"sc{ti}_{c}")
                S.op("dve", lambda v, c=c: v.tensor_copy(out=carry.f[:, c:c + 1], in_=cum.f[:, c * TW + TW - 1:c * TW + TW]), r=[cum.tok], w=[carry.tok], dbg=f"cr{ti}_{c}")
                S.op("dve", lambda v, slc=slc: v.tensor_tensor(out=cum.f[:, slc], in0=cum.f[:, slc], in1=dpl.f[:, TW:2 * TW], op=ALU.mult), r=[cum.tok, dpl.tok], w=[cum.tok], dbg=f"me{ti}_{c}")
                S.op("dve", lambda v, slc=slc, c=c: v.tensor_tensor(out=ob.b[:, slc], in0=cum.f[:, slc], in1=kvv[:, c, 1:1 + TW], op=ALU.mult), r=[cum.tok, kpad.tok], w=[ob.tok], dbg=f"oa{ti}_{c}")
            for oc in range(DC):
                for kc in range(DC):
                    S.op("pe", lambda t, oc=oc, kc=kc: t.matmul(self.pb[5][:, 0:TW], wo.b[:, kc * D + oc * P:kc * D + (oc + 1) * P], ob.b[:, kc * TW:(kc + 1) * TW], start=(kc == 0), stop=(kc == DC - 1)), r=[ob.tok, wo.tok], w=[self.pbt[5]], dbg=f"om{ti}_{oc}")
                S.op("dve", lambda v, oc=oc: v.tensor_copy(out=s1.f[:, oc * TW:(oc + 1) * TW], in_=self.pb[5][:, 0:TW]), r=[self.pbt[5]], w=[s1.tok], dbg=f"oe{ti}_{oc}")
            self.ln_stats(s1.f, s1.tok, sqr, TW, f"o{ti}")
            S.op("dve", lambda v: v.tensor_tensor(out=R.fr(0, 1, TW, 2 * TW), in0=R.fr(0, 1, 0, TW), in1=R.fr(0, 1, 3 * TW, 4 * TW), op=ALU.mult), r=[R.tok], w=[R.tok], dbg=f"omr{ti}")
            self.bcast(R.fr(0, 1, 3 * TW, 4 * TW), 2, TW, [R.tok], f"orp{ti}")
            self.bcast(R.fr(0, 1, TW, 2 * TW), 3, TW, [R.tok], f"omp{ti}")
            for c in range(DC):
                slc = slice(c * TW, (c + 1) * TW)
                S.op("dve", lambda v, slc=slc: v.tensor_tensor(out=s1.f[:, slc], in0=s1.f[:, slc], in1=self.pb[2][:, 0:TW], op=ALU.mult), r=[s1.tok, self.pbt[2]], w=[s1.tok], dbg=f"na{ti}_{c}")
            for c in range(DC):
                slc = slice(c * TW, (c + 1) * TW)
                S.op("dve", lambda v, slc=slc: v.tensor_tensor(out=s1.f[:, slc], in0=s1.f[:, slc], in1=self.pb[3][:, 0:TW], op=ALU.subtract), r=[s1.tok, self.pbt[3]], w=[s1.tok], dbg=f"nb{ti}_{c}")
            for c in range(DC):
                slc = slice(c * TW, (c + 1) * TW)
                S.op("dve", lambda v, slc=slc, c=c, xmt=xmt: v.scalar_tensor_tensor(out=xmt.f[:, slc], in0=s1.f[:, slc], scalar=cc["g1p"].f[:, c:c + 1], in1=xh1.f[:, slc], op0=ALU.mult, op1=ALU.add), r=[s1.tok, xh1.tok, cc["g1p"].tok], w=[xmt.tok], dbg=f"nc{ti}_{c}")
            for c in range(DC):
                slc = slice(c * TW, (c + 1) * TW)
                S.op("dve", lambda v, slc=slc, c=c, xmt=xmt: v.tensor_scalar(out=xmt.f[:, slc], in0=xmt.f[:, slc], scalar1=cc["C1"].f[:, c:c + 1], scalar2=None, op0=ALU.add), r=[xmt.tok, cc["C1"].tok], w=[xmt.tok], dbg=f"nd{ti}_{c}")
            S.dma("adma", hb["xmid"].rearrange("(c p) t -> p c t", p=P)[:, :, t0:t0 + TW],
                  xmt.f.rearrange("p (c t) -> p c t", c=DC), r=[xmt.tok],
                  w=[self.htok["xmid"]], dbg=f"xmo{ti}")
        A.free(wqk, wv, wg, wo, *xin, xh1, cbuf, xcb, kpad, vbuf, gbuf, cosb,
               s1, sqr, ob, *xmid, dpl, carry)

    def p2_mlp(self):
        S, hb, A, cc = self.S, self.hb, self.A, self.cc
        W = TW2

        wgate = A.alloc("p2_wg", DC * FF // 2)
        wup = A.alloc("p2_wu", DC * FF // 2)
        biasg = A.alloc("p2_bg", FC)
        biasu = A.alloc("p2_bu", FC)
        lnbb = A.alloc("p2_lnbb", 8)
        S.op("dve", lambda v: v.tensor_copy(out=lnbb.b[:, 0:DC], in_=cc["ln2_b"].f),
             r=[cc["ln2_b"].tok], w=[lnbb.tok], dbg="p2lnbb")
        sl = A.alloc("p2_slab", 2 * FF)
        stok = [object(), object()]
        for wi, (nm, dst, bias) in enumerate((("gate_wT", wgate, biasg), ("up_wT", wup, biasu))):
            for kc in range(DC):
                par = kc % 2
                dsl = sl.f[:, par * FF:(par + 1) * FF]
                S.dma("sdma", dsl, hb[nm][kc * P:(kc + 1) * P, :], w=[stok[par]], dbg=f"p2l{nm}{kc}")
                S.op("dve", lambda v, kc=kc, dsl=dsl, dst=dst: v.tensor_copy(
                     out=dst.b[:, kc * FF:(kc + 1) * FF], in0=dsl) if False else v.tensor_copy(
                     out=dst.b[:, kc * FF:(kc + 1) * FF], in_=dsl),
                     r=[stok[par]], w=[dst.tok], dbg=f"p2f{nm}{kc}")
            for oc in range(FC):
                for kc in range(DC):
                    S.op("pe", lambda t, kc=kc, oc=oc, dst=dst: t.matmul(
                         self.pb[6][:, oc:oc + 1],
                         dst.b[:, kc * FF + oc * P:kc * FF + (oc + 1) * P],
                         lnbb.b[:, kc:kc + 1],
                         start=(kc == 0), stop=(kc == DC - 1)),
                         r=[dst.tok, lnbb.tok], w=[self.pbt[6]], dbg=f"p2b{nm}{kc}_{oc}")
            S.op("dve", lambda v, bias=bias: v.tensor_copy(out=bias.f[:, 0:FC], in_=self.pb[6][:, 0:FC]), r=[self.pbt[6]], w=[bias.tok], dbg=f"p2bc{wi}")
        A.free(sl)
        S.barrier()

        xm = A.alloc("p2_xm", DC * W)
        xhb = A.alloc("p2_xhb", DC * W // 2)
        sqr = A.alloc("p2_sq", W)
        sg = A.alloc("p2_sg", W)
        hbc = A.alloc("p2_hbc", W)
        R = self.rows

        for ti in range(NT2):
            t0 = ti * W
            S.dma("sdma", xm.f.rearrange("p (c t) -> p c t", c=DC),
                  hb["xmid"].rearrange("(c p) t -> p c t", p=P)[:, :, t0:t0 + W],
                  r=[self.htok["xmid"]], w=[xm.tok], dbg=f"p2x{ti}")
            self.ln_stats(xm.f, xm.tok, sqr, W, f"p2l{ti}")
            self.ln_apply(xm.f, xm.tok, xhb.b, xhb.tok, W, f"p2a{ti}", tmp=sqr,
                          scale_col=(cc["ln2_w"], None))
            for oc in range(FC):
                for kc in range(DC):
                    S.op("pe", lambda t, oc=oc, kc=kc: t.matmul(self.pb[5][:, 0:W], wgate.b[:, kc * FF + oc * P:kc * FF + (oc + 1) * P], xhb.b[:, kc * W:(kc + 1) * W], start=(kc == 0), stop=(kc == DC - 1)), r=[xhb.tok, wgate.tok], w=[self.pbt[5]], dbg=f"p2g{ti}_{oc}")
                S.op("act", lambda s, oc=oc: s.activation(out=sg.f[:, 0:W], in_=self.pb[5][:, 0:W], func=AF.Silu, bias=biasg.f[:, oc:oc + 1]), r=[self.pbt[5], biasg.tok], w=[sg.tok], dbg=f"p2s{ti}_{oc}")
                for kc in range(DC):
                    S.op("pe", lambda t, oc=oc, kc=kc: t.matmul(self.pb[6][:, 0:W], wup.b[:, kc * FF + oc * P:kc * FF + (oc + 1) * P], xhb.b[:, kc * W:(kc + 1) * W], start=(kc == 0), stop=(kc == DC - 1)), r=[xhb.tok, wup.tok], w=[self.pbt[6]], dbg=f"p2u{ti}_{oc}")
                S.op("dve", lambda v, oc=oc: v.scalar_tensor_tensor(out=hbc.b[:, 0:W], in0=self.pb[6][:, 0:W], scalar=biasu.f[:, oc:oc + 1], in1=sg.f[:, 0:W], op0=ALU.add, op1=ALU.mult), r=[self.pbt[6], biasu.tok, sg.tok], w=[hbc.tok], dbg=f"p2h{ti}_{oc}")
                S.dma("sdma", hb["hbuf"][:, oc * T + t0:oc * T + t0 + W], hbc.b[:, 0:W], r=[hbc.tok], w=[self.htok["hbuf"]], dbg=f"p2o{ti}_{oc}")
        A.free(wgate, wup, biasg, biasu, lnbb, xm, xhb, sqr, sg, hbc)
        S.barrier()

        wdn = A.alloc("p2_wdn", FC * D // 2)
        self.preload_w2big("down_wT", wdn, "p2d")
        hbt = A.alloc("p2_hbt", FC * W // 2)
        xm2 = A.alloc("p2_xm2", DC * W)
        xo = A.alloc("p2_xo", DC * W)
        x1t = A.alloc("p2_x1t", DC * W)
        for ti in range(NT2):
            t0 = ti * W
            for hc in range(FC):
                S.dma("sdma", hbt.b[:, hc * W:(hc + 1) * W],
                      hb["hbuf"][:, hc * T + t0:hc * T + t0 + W],
                      r=[self.htok["hbuf"]], w=[hbt.tok], dbg=f"p2hi{ti}_{hc}")
            S.dma("sdma", xm2.f.rearrange("p (c t) -> p c t", c=DC),
                  hb["xmid"].rearrange("(c p) t -> p c t", p=P)[:, :, t0:t0 + W],
                  r=[self.htok["xmid"]], w=[xm2.tok], dbg=f"p2x2{ti}")
            S.dma("adma", xo.f.rearrange("p (c t) -> p c t", c=DC),
                  hb["x_fm"].rearrange("(c p) t -> p c t", p=P)[:, :, t0:t0 + W],
                  w=[xo.tok], dbg=f"p2xo{ti}")
            for c in range(DC):
                S.op("pool", lambda g, c=c: g.tensor_tensor(out=x1t.f[:, c * W:(c + 1) * W], in0=xo.f[:, c * W:(c + 1) * W], in1=xm2.f[:, c * W:(c + 1) * W], op=ALU.add), r=[xo.tok, xm2.tok], w=[x1t.tok], dbg=f"p2p{ti}_{c}")
            for dc_ in range(DC):
                bank = 5
                for kc in range(FC):
                    S.op("pe", lambda t, dc_=dc_, kc=kc, bank=bank: t.matmul(self.pb[bank][:, 0:W], wdn.b[:, kc * D + dc_ * P:kc * D + (dc_ + 1) * P], hbt.b[:, kc * W:(kc + 1) * W], start=(kc == 0), stop=(kc == FC - 1)), r=[hbt.tok, wdn.tok], w=[self.pbt[bank]], dbg=f"p2d{ti}_{dc_}")
                S.op("dve", lambda v, dc_=dc_, bank=bank: v.scalar_tensor_tensor(out=x1t.f[:, dc_ * W:(dc_ + 1) * W], in0=self.pb[bank][:, 0:W], scalar=cc["gamma2"].f[:, dc_:dc_ + 1], in1=x1t.f[:, dc_ * W:(dc_ + 1) * W], op0=ALU.mult, op1=ALU.add), r=[self.pbt[bank], cc["gamma2"].tok, x1t.tok], w=[x1t.tok], dbg=f"p2e{ti}_{dc_}")
            S.dma("sdma", hb["x1"].rearrange("(c p) t -> p c t", p=P)[:, :, t0:t0 + W],
                  x1t.f.rearrange("p (c t) -> p c t", c=DC), r=[x1t.tok],
                  w=[self.htok["x1"]], dbg=f"p2y{ti}")
        A.free(wdn, hbt, xm2, xo, x1t)

    def preload_w2big(self, nm, dst, pref):
        S, hb, A = self.S, self.hb, self.A
        sl = A.alloc(pref + "_slab", 2 * 2048)
        stok = [object(), object()]
        for q in range(FC // 2):
            par = q % 2
            dsl = sl.f[:, par * 2048:(par + 1) * 2048]
            S.dma("sdma", dsl.rearrange("p (c o) -> p c o", c=2),
                  hb[nm].rearrange("(c p) o -> p c o", p=P)[:, 2 * q:2 * q + 2, :],
                  w=[stok[par]], dbg=f"{pref}l{q}")
            S.op("dve", lambda v, q=q, dsl=dsl: v.tensor_copy(
                 out=dst.b[:, q * 2048:(q + 1) * 2048], in_=dsl), r=[stok[par]],
                 w=[dst.tok], dbg=f"{pref}c{q}")
        A.free(sl)

    def p3_ffn(self, src, dst, tag):
        S, hb, A, cc = self.S, self.hb, self.A, self.cc
        W = TW2
        stok = self.htok.get(src)
        w1 = A.alloc(tag + "_w1", DC * FF // 2)
        S.dma("sdma", w1.b[:, 0:DC * FF].rearrange("p (c o) -> p c o", c=DC),
              hb["W1"].rearrange("(c p) o -> p c o", p=P), r=[self.htok["W1"]],
              w=[w1.tok], dbg=tag + "lw1")
        xt = [A.alloc(f"{tag}_x{i}", DC * W) for i in range(2)]
        xhb = A.alloc(tag + "_xhb", DC * W // 2)
        sqr = A.alloc(tag + "_sq", W)
        hgc = A.alloc(tag + "_hgc", W)
        for ti in range(NT2):
            x_ = xt[ti % 2]
            t0 = ti * W
            rd = [stok] if stok else []
            S.dma("sdma", x_.f.rearrange("p (c t) -> p c t", c=DC),
                  hb[src].rearrange("(c p) t -> p c t", p=P)[:, :, t0:t0 + W],
                  r=rd, w=[x_.tok], dbg=f"{tag}x{ti}")
            self.ln_stats(x_.f, x_.tok, sqr, W, f"{tag}l{ti}")
            self.ln_apply(x_.f, x_.tok, xhb.b, xhb.tok, W, f"{tag}a{ti}", tmp=sqr)
            for oc in range(FC):
                bank = 5 if oc % 2 == 0 else 6
                for kc in range(DC):
                    S.op("pe", lambda t, oc=oc, kc=kc, bank=bank: t.matmul(self.pb[bank][:, 0:W], w1.b[:, kc * FF + oc * P:kc * FF + (oc + 1) * P], xhb.b[:, kc * W:(kc + 1) * W], start=(kc == 0), stop=(kc == DC - 1)), r=[xhb.tok, w1.tok], w=[self.pbt[bank]], dbg=f"{tag}m{ti}_{oc}")
                S.op("act", lambda s, oc=oc, bank=bank: s.activation(out=hgc.b[:, 0:W], in_=self.pb[bank][:, 0:W], func=AF.Gelu, bias=cc["bias1"].f[:, oc:oc + 1]), r=[self.pbt[bank], cc["bias1"].tok], w=[hgc.tok], dbg=f"{tag}g{ti}_{oc}")
                S.dma("adma", hb["hbuf"][:, oc * T + t0:oc * T + t0 + W], hgc.b[:, 0:W], r=[hgc.tok], w=[self.htok["hbuf"]], dbg=f"{tag}ho{ti}_{oc}")
        A.free(w1, xhb, sqr, hgc)
        S.barrier()
        w2 = A.alloc(tag + "_w2", FC * D // 2)
        S.dma("sdma", w2.b[:, 0:FC * D].rearrange("p (c o) -> p c o", c=FC),
              hb["W2"].rearrange("(c p) o -> p c o", p=P), r=[self.htok["W2"]],
              w=[w2.tok], dbg=tag + "lw2")
        hgt = A.alloc(tag + "_hgt", FC * W // 2)
        yt = A.alloc(tag + "_yt", DC * W)
        for ti in range(NT2):
            x_ = xt[ti % 2]
            t0 = ti * W
            rd = [stok] if stok else []
            for hc in range(FC):
                S.dma("sdma", hgt.b[:, hc * W:(hc + 1) * W],
                      hb["hbuf"][:, hc * T + t0:hc * T + t0 + W],
                      r=[self.htok["hbuf"]], w=[hgt.tok], dbg=f"{tag}hi{ti}_{hc}")
            S.dma("adma", x_.f.rearrange("p (c t) -> p c t", c=DC),
                  hb[src].rearrange("(c p) t -> p c t", p=P)[:, :, t0:t0 + W],
                  r=rd, w=[x_.tok], dbg=f"{tag}x2{ti}")
            for dc_ in range(DC):
                bank = 5 if dc_ % 2 == 0 else 6
                for kc in range(FC):
                    S.op("pe", lambda t, dc_=dc_, kc=kc, bank=bank: t.matmul(self.pb[bank][:, 0:W], w2.b[:, kc * D + dc_ * P:kc * D + (dc_ + 1) * P], hgt.b[:, kc * W:(kc + 1) * W], start=(kc == 0), stop=(kc == FC - 1)), r=[hgt.tok, w2.tok], w=[self.pbt[bank]], dbg=f"{tag}n{ti}_{dc_}")
                S.op("dve", lambda v, dc_=dc_, bank=bank, x_=x_: v.scalar_tensor_tensor(out=yt.f[:, dc_ * W:(dc_ + 1) * W], in0=self.pb[bank][:, 0:W], scalar=cc["bias2"].f[:, dc_:dc_ + 1], in1=x_.f[:, dc_ * W:(dc_ + 1) * W], op0=ALU.add, op1=ALU.add), r=[self.pbt[bank], cc["bias2"].tok, x_.tok], w=[yt.tok], dbg=f"{tag}e{ti}_{dc_}")
            S.dma("sdma", hb[dst].rearrange("(c p) t -> p c t", p=P)[:, :, t0:t0 + W],
                  yt.f.rearrange("p (c t) -> p c t", c=DC), r=[yt.tok],
                  w=[self.htok[dst]], dbg=f"{tag}y{ti}")
        A.free(w2, hgt, yt, *xt)

    def p4_attnb(self):
        S, hb, A, cc = self.S, self.hb, self.A, self.cc
        wq = A.alloc("p4_wq", DC * D // 2)
        wk = A.alloc("p4_wk", DC * D // 2)
        wv_ = A.alloc("p4_wv", DC * D // 2)
        wo = A.alloc("p4_wo", DC * D // 2)
        self.preload_w(("WqT", "WkT", "WvT", "WoT"), (wq, wk, wv_, wo), "p4")
        bvr = A.alloc("p4_bvr", D)
        S.dma("sdma", bvr.fr(0, 1, 0, D), hb["bv_row"], w=[bvr.tok], dbg="p4bv")
        bkr = A.alloc("p4_bkr", D)
        S.dma("sdma", bkr.fr(0, 1, 0, D), hb["bk_row"], w=[bkr.tok], dbg="p4bk")

        xt = [A.alloc("p4_x0", DC * TW)] * 2
        xhb = A.alloc("p4_xhb", DC * TW // 2)
        sqr = A.alloc("p4_sq", TW)
        sqf = A.alloc("p4_sqf", DC * TW)
        qf = A.alloc("p4_qf", DC * TW)
        kf = A.alloc("p4_kf", DC * TW)
        qf2 = A.alloc("p4_qf2", DC * TW)
        kf2 = A.alloc("p4_kf2", DC * TW)
        ktok = A.alloc("p4_kt", 2 * D)
        vo = A.alloc("p4_vo", 2 * NH * 66 + 8)
        sT = A.alloc("p4_sT", 4 * P)
        att = A.alloc("p4_att", D)
        atf = A.alloc("p4_atf", DC * TW // 2)
        dcol = A.alloc("p4_dcol", 4)
        Mst = A.alloc("p4_M", NH * 66 + 8)
        yt = A.alloc("p4_yt", DC * TW)
        R = self.rows

        if self.dbg_name == "att":
            self.dbg_att = self.nc.dram_tensor("dbg_att", [P, 2 * D], BF16, kind="ExternalOutput").ap()
        S.op("dve", lambda v: v.memset(Mst.fr(0, 64, 0, NH * 66), 0.0), w=[Mst.tok], dbg="M0")
        S.op("dve", lambda v: v.memset(
            vo.f[:, 0:2 * NH * 66].rearrange("p (c h e) -> p c h e", c=2, h=NH)[:, :, :, 64:66],
            1.0), w=[vo.tok], dbg="vo1")

        def softplus(ap, tok_, wd, pref):
            S.op("act", lambda s: s.activation(out=sqf.f[:, 0:wd], in_=ap, func=AF.Abs), r=[tok_], w=[sqf.tok], dbg=pref + "ab")
            S.op("act", lambda s: s.activation(out=sqf.f[:, 0:wd], in_=sqf.f[:, 0:wd], func=AF.Exp, scale=cc["hlp"].f[:, 1:2]), r=[sqf.tok, cc["hlp"].tok], w=[sqf.tok], dbg=pref + "ex")
            S.op("act", lambda s: s.activation(out=sqf.f[:, 0:wd], in_=sqf.f[:, 0:wd], func=AF.Ln, bias=cc["ones_col"].f[:, 0:1]), r=[sqf.tok, cc["ones_col"].tok], w=[sqf.tok], dbg=pref + "ln")
            S.op("dve", lambda v: v.tensor_scalar(out=ap, in0=ap, scalar1=0.0, scalar2=None, op0=ALU.max), r=[tok_], w=[tok_], dbg=pref + "re")
            S.op("dve", lambda v: v.tensor_tensor(out=ap, in0=ap, in1=sqf.f[:, 0:wd], op=ALU.add), r=[tok_, sqf.tok], w=[tok_], dbg=pref + "ad")

        for ti in range(NT):
            x_ = xt[ti % 2]
            t0 = ti * TW
            S.dma("sdma", x_.f.rearrange("p (c t) -> p c t", c=DC),
                  hb["x2"].rearrange("(c p) t -> p c t", p=P)[:, :, t0:t0 + TW],
                  r=[self.htok["x2"]], w=[x_.tok], dbg=f"p4x{ti}")
            self.ln_stats(x_.f, x_.tok, sqr, TW, f"p4l{ti}")
            self.ln_apply(x_.f, x_.tok, xhb.b, xhb.tok, TW, f"p4a{ti}", tmp=sqr)
            for nm, wres, dst_, bias in (("q", wq, qf, "bq"), ("k", wk, kf, "bk")):
                for oc in range(DC):
                    for kc in range(DC):
                        S.op("pe", lambda t, oc=oc, kc=kc, wres=wres: t.matmul(self.pb[5][:, 0:TW], wres.b[:, kc * D + oc * P:kc * D + (oc + 1) * P], xhb.b[:, kc * TW:(kc + 1) * TW], start=(kc == 0), stop=(kc == DC - 1)), r=[xhb.tok, wres.tok], w=[self.pbt[5]], dbg=f"p4{nm}m{ti}_{oc}")
                    S.op("dve", lambda v, oc=oc, dst_=dst_, bias=bias: v.tensor_scalar(out=dst_.f[:, oc * TW:(oc + 1) * TW], in0=self.pb[5][:, 0:TW], scalar1=cc[bias].f[:, oc:oc + 1], scalar2=None, op0=ALU.add), r=[self.pbt[5], cc[bias].tok], w=[dst_.tok], dbg=f"p4{nm}e{ti}_{oc}")
            softplus(qf.f, qf.tok, DC * TW, f"p4sq{ti}")
            softplus(kf.f, kf.tok, DC * TW, f"p4sk{ti}")
            S.dma("sdma", qf2.fr(0, 64, 0, DC * TW), qf.fr(64, 128, 0, DC * TW),
                  r=[qf.tok], w=[qf2.tok], dbg=f"p4q2{ti}")
            S.dma("sdma", kf2.fr(0, 64, 0, DC * TW), kf.fr(64, 128, 0, DC * TW),
                  r=[kf.tok], w=[kf2.tok], dbg=f"p4k2{ti}")
            for tc in range(2):
                for nm, wres in (("v", wv_), ("k2", wk)):
                    for nhalf in range(2):
                        for kc in range(DC):
                            S.op("pe", lambda t, tc=tc, nhalf=nhalf, kc=kc, wres=wres, nm=nm: t.matmul(
                                self.pb[6][:, 0:512],
                                xhb.b[:, kc * TW + tc * P:kc * TW + (tc + 1) * P],
                                wres.b[:, kc * D + nhalf * 512:kc * D + (nhalf + 1) * 512],
                                start=(kc == 0), stop=(kc == DC - 1 and nm == "k2")),
                                r=[xhb.tok, wres.tok], w=[self.pbt[6]], dbg=f"p4t{nm}{ti}_{tc}_{nhalf}")
                        if nm == "v":
                            S.op("pe", lambda t, nhalf=nhalf: t.matmul(self.pb[6][:, 0:512], cc["ones_row"].fr(0, 1, 0, P), bvr.fr(0, 1, nhalf * 512, (nhalf + 1) * 512), start=False, stop=True), r=[cc["ones_row"].tok, bvr.tok], w=[self.pbt[6]], dbg=f"p4bv{ti}_{tc}_{nhalf}")
                            S.op("dve", lambda v, tc=tc, nhalf=nhalf: v.tensor_copy(
                                out=vo.f[:, 0:2 * NH * 66].rearrange("p (c h e) -> p c h e", c=2, h=NH)[:, tc, nhalf * 8:(nhalf + 1) * 8, 0:64],
                                in_=self.pb[6][:, 0:512].rearrange("p (h e) -> p h e", h=8)),
                                r=[self.pbt[6]], w=[vo.tok], dbg=f"p4ve{ti}_{tc}_{nhalf}")
                        else:
                            S.op("pe", lambda t, nhalf=nhalf: t.matmul(self.pb[6][:, 0:512], cc["ones_row"].fr(0, 1, 0, P), bkr.fr(0, 1, nhalf * 512, (nhalf + 1) * 512), start=False, stop=True), r=[cc["ones_row"].tok, bkr.tok], w=[self.pbt[6]], dbg=f"p4bk{ti}_{tc}_{nhalf}")
                            S.op("dve", lambda v, tc=tc, nhalf=nhalf: v.tensor_copy(out=ktok.f[:, tc * D + nhalf * 512:tc * D + (nhalf + 1) * 512], in_=self.pb[6][:, 0:512]), r=[self.pbt[6]], w=[ktok.tok], dbg=f"p4ke{ti}_{tc}_{nhalf}")
            softplus(ktok.f, ktok.tok, 2 * D, f"p4s2{ti}")
            SW = 66
            for tc in range(2):
                for grp in range(4):
                    for hh in range(4):
                        h = grp * 4 + hh
                        ksrc = kf if h % 2 == 0 else kf2
                        qsrc = qf if h % 2 == 0 else qf2
                        chs = h // 2
                        S.op("pe", lambda t, hh=hh, chs=chs, tc=tc, ksrc=ksrc, qsrc=qsrc: t.matmul(
                            self.pb[0][:, hh * P:(hh + 1) * P],
                            ksrc.fr(0, 64, chs * TW + tc * P, chs * TW + (tc + 1) * P),
                            qsrc.fr(0, 64, chs * TW + tc * P, chs * TW + (tc + 1) * P),
                            start=True, stop=True), r=[ksrc.tok, qsrc.tok], w=[self.pbt[0]], dbg=f"p4S{ti}_{tc}_{h}")
                    S.op("dve", lambda v: v.tensor_tensor(out=sT.f, in0=self.pb[0][:, 0:4 * P], in1=cc["mask4"].f, op=ALU.mult), r=[self.pbt[0], cc["mask4"].tok], w=[sT.tok], dbg=f"p4m{ti}_{tc}_{grp}")
                    for hh in range(4):
                        h = grp * 4 + hh
                        qsrc = qf if h % 2 == 0 else qf2
                        chs = h // 2
                        S.op("pe", lambda t, h=h, hh=hh, tc=tc: t.matmul(
                            self.pb[1][:, hh * SW:hh * SW + SW], sT.f[:, hh * P:(hh + 1) * P],
                            vo.f[:, (tc * NH + h) * SW:(tc * NH + h + 1) * SW],
                            start=True, stop=False), r=[sT.tok, vo.tok], w=[self.pbt[1]], dbg=f"p4n{ti}_{tc}_{h}")
                        S.op("pe", lambda t, h=h, hh=hh, chs=chs, tc=tc, qsrc=qsrc: t.matmul(
                            self.pb[1][:, hh * SW:hh * SW + SW],
                            qsrc.fr(0, 64, chs * TW + tc * P, chs * TW + (tc + 1) * P),
                            Mst.fr(0, 64, h * SW, (h + 1) * SW),
                            start=False, stop=True),
                            r=[qsrc.tok, Mst.tok], w=[self.pbt[1]], dbg=f"p4i{ti}_{tc}_{h}")
                        S.op("pe", lambda t, h=h, hh=hh, tc=tc: t.matmul(
                            self.pb[4][0:64, hh * SW:hh * SW + SW],
                            ktok.f[:, tc * D + h * 64:tc * D + (h + 1) * 64],
                            vo.f[:, (tc * NH + h) * SW:(tc * NH + h + 1) * SW],
                            start=True, stop=True),
                            r=[ktok.tok, vo.tok], w=[self.pbt[4]], dbg=f"p4u{ti}_{tc}_{h}")
                    for hh in range(4):
                        h = grp * 4 + hh
                        S.op("dve", lambda v, hh=hh: v.reciprocal(out=dcol.f[:, 0:1], in_=self.pb[1][:, hh * SW + 64:hh * SW + 65]), r=[self.pbt[1]], w=[dcol.tok], dbg=f"p4r{ti}_{tc}_{h}")
                        S.op("dve", lambda v, h=h, hh=hh, tc=tc: v.tensor_scalar(out=att.b[:, tc * D + h * 64:tc * D + (h + 1) * 64], in0=self.pb[1][:, hh * SW:hh * SW + 64], scalar1=dcol.f[:, 0:1], scalar2=None, op0=ALU.mult), r=[self.pbt[1], dcol.tok], w=[att.tok], dbg=f"p4d{ti}_{tc}_{h}")
                        S.op("dve", lambda v, h=h, hh=hh: v.tensor_tensor(out=Mst.fr(0, 64, h * SW, (h + 1) * SW), in0=Mst.fr(0, 64, h * SW, (h + 1) * SW), in1=self.pb[4][0:64, hh * SW:hh * SW + SW], op=ALU.add), r=[Mst.tok, self.pbt[4]], w=[Mst.tok], dbg=f"p4M{ti}_{tc}_{h}")
            if self.dbg_name == "att" and ti == 0:
                S.dma("sdma", self.dbg_att, att.b[:, 0:2 * D], r=[att.tok], w=[self.htok["y"]], dbg="attdmp")
            for tc in range(2):
                for oc in range(DC):
                    S.op("pe", lambda t, tc=tc, oc=oc: t.transpose(self.pbh[:, 0:P], att.b[:, tc * D + oc * P:tc * D + (oc + 1) * P], cc["identb"].b[:, 0:P]), r=[att.tok, cc["identb"].tok], w=[self.pbht], dbg=f"p4t{ti}_{tc}_{oc}")
                    S.op("dve", lambda v, tc=tc, oc=oc: v.tensor_copy(out=atf.b[:, oc * TW + tc * P:oc * TW + (tc + 1) * P], in_=self.pbh[:, 0:P]), r=[self.pbht], w=[atf.tok], dbg=f"p4tc{ti}_{tc}_{oc}")
            for oc in range(DC):
                for kc in range(DC):
                    S.op("pe", lambda t, oc=oc, kc=kc: t.matmul(self.pb[5][:, 0:TW], wo.b[:, kc * D + oc * P:kc * D + (oc + 1) * P], atf.b[:, kc * TW:(kc + 1) * TW], start=(kc == 0), stop=(kc == DC - 1)), r=[atf.tok, wo.tok], w=[self.pbt[5]], dbg=f"p4o{ti}_{oc}")
                S.op("dve", lambda v, oc=oc, x_=x_: v.scalar_tensor_tensor(out=yt.f[:, oc * TW:(oc + 1) * TW], in0=self.pb[5][:, 0:TW], scalar=cc["bo"].f[:, oc:oc + 1], in1=x_.f[:, oc * TW:(oc + 1) * TW], op0=ALU.add, op1=ALU.add), r=[self.pbt[5], cc["bo"].tok, x_.tok], w=[yt.tok], dbg=f"p4y{ti}_{oc}")
            S.dma("adma", hb["x3"].rearrange("(c p) t -> p c t", p=P)[:, :, t0:t0 + TW],
                  yt.f.rearrange("p (c t) -> p c t", c=DC), r=[yt.tok],
                  w=[self.htok["x3"]], dbg=f"p4yo{ti}")
        A.free(wq, wk, wv_, wo, bvr, bkr, xt[0], xhb, sqr, sqf, qf, kf, qf2, kf2,
               ktok, vo, sT, att, atf, dcol, Mst, yt)

    def finish(self):
        S, hb = self.S, self.hb
        if self.stop is not None:
            srcmap = {"p1": "xmid", "p2": "x1", "p3": "x2", "p4": "x3"}
            nm = srcmap[self.stop]
            S.dma("sdma", hb["y"], hb[nm], r=[self.htok[nm]], w=[self.htok["y"]], dbg="ycp")
        S.op("dve", lambda v: v.nop(), r=[self.htok["y"]], dbg="final")


def _run_built(pr, in_maps, core_ids, **kw):
    pr.S.assign()
    with ExitStack() as st:
        sems = {}
        for s in Sched.SEMS:
            sems[s] = st.enter_context(pr.nc.semaphore("sem_" + s))
        with pr.nc.Block() as block:
            @block.tensor
            def _(tensor):
                pr.S.run_engine("pe", tensor, sems)

            @block.vector
            def _(vector):
                pr.S.run_engine("dve", vector, sems)

            @block.scalar
            def _(scalar):
                pr.S.run_engine("act", scalar, sems)

            @block.gpsimd
            def _(gpsimd):
                pr.S.run_engine("pool", gpsimd, sems)

            @block.sync
            def _(sync):
                pr.S.run_engine("sync", sync, sems)
        return run_bass_kernel_spmd(pr.nc, in_maps, core_ids=core_ids, **kw)


def kernel(**inputs):
    pr = Prog(_DEV_STOP, _DEV_DBG)
    with pr.stack:
        pr.build()
        shared = _prep_shared(inputs)
        x = np.asarray(inputs["L0_attn_a_x"], np.float32)
        in_maps = []
        for core in range(8):
            m = dict(shared)
            m["x_fm"] = np.ascontiguousarray(x[core].T)
            in_maps.append(m)
        res = _run_built(pr, in_maps, list(range(8)))
    out = np.empty((8, T, D), np.float32)
    for core in range(8):
        out[core] = res.results[core]["y"].T
    return out


# version 04
# speedup vs baseline: 1.2059x; 1.2059x over previous
import numpy as np
import ml_dtypes
import concourse.bass as bass
import concourse.mybir as mybir
from concourse.bass_utils import run_bass_kernel_spmd
from concourse.alu_op_type import AluOpType as ALU
from contextlib import ExitStack

F32 = mybir.dt.float32
BF16 = mybir.dt.bfloat16
AF = mybir.ActivationFunctionType
AX = mybir.AxisListType

P = 128
T = 2048
D = 1024
FF = 4096
NH = 16
DC = D // P
FC = FF // P
TW = 256
NT = T // TW
TW2 = 512
NT2 = T // TW2
ARENA_W = 51200

_DEV_STOP = None
_DEV_DBG = None


class _Op:
    __slots__ = ("ieng", "sem", "fn", "deps", "tick", "need_inc", "dbg", "is_dma",
                 "is_barrier")

    def __init__(self, ieng, sem, fn, dbg="", is_dma=False, is_barrier=False):
        self.ieng = ieng
        self.sem = sem
        self.fn = fn
        self.deps = []
        self.tick = None
        self.need_inc = False
        self.dbg = dbg
        self.is_dma = is_dma
        self.is_barrier = is_barrier


class Sched:
    IENG = ("pe", "dve", "act", "pool", "sync")
    SEMS = ("pe", "dve", "act", "pool", "sdma", "adma")

    def __init__(self):
        self.by_eng = {e: [] for e in self.IENG}
        self.last_w = {}
        self.readers = {}
        self.last_real = {}

    def _add(self, o, r, w):
        deps = []
        for b in r:
            lw = self.last_w.get(id(b))
            if lw is not None and not (lw.ieng == o.ieng == "pe"):
                deps.append(lw)
        for b in w:
            lw = self.last_w.get(id(b))
            dma = o.sem in ("sdma", "adma")
            if lw is not None and (lw.ieng != o.ieng or dma or lw.sem in ("sdma", "adma")):
                deps.append(lw)
            for rd in self.readers.get(id(b), ()):
                if rd.ieng != o.ieng or dma or rd.sem in ("sdma", "adma"):
                    deps.append(rd)
        seen = set()
        for d in deps:
            if d is not o and id(d) not in seen:
                seen.add(id(d))
                o.deps.append(d)
                d.need_inc = True
        for b in r:
            self.readers.setdefault(id(b), []).append(o)
        for b in w:
            self.last_w[id(b)] = o
            self.readers[id(b)] = []
        self.by_eng[o.ieng].append(o)
        self.last_real[o.ieng] = o
        return o

    def op(self, eng, fn, r=(), w=(), dbg=""):
        return self._add(_Op(eng, eng, fn, dbg), r, w)

    def dma(self, q, out_ap, in_ap, r=(), w=(), dbg=""):
        ieng = "sync" if q == "sdma" else "act"
        o = _Op(ieng, q, lambda e, o_=out_ap, i_=in_ap: e.dma_start(out=o_, in_=i_),
                dbg, is_dma=True)
        return self._add(o, r, w)

    def barrier(self):
        lasts = dict(self.last_real)
        for e in self.IENG:
            deps = [op for f, op in lasts.items() if op is not None and f != e]
            if not deps:
                continue
            o = _Op(e, e, lambda eng: eng.nop(), "barrier", is_barrier=True)
            for d in deps:
                o.deps.append(d)
                d.need_inc = True
            self.by_eng[e].append(o)

    def assign(self):
        cnt = {s: 0 for s in self.SEMS}
        for e in self.IENG:
            for o in self.by_eng[e]:
                if o.is_dma:
                    o.need_inc = True
                assert not (o.is_barrier and o.need_inc), "dep on barrier nop"
                if o.need_inc:
                    cnt[o.sem] += 16 if o.is_dma else 1
                    o.tick = cnt[o.sem]

    def run_engine(self, ieng, eng_obj, sems):
        hw = {}
        for o in self.by_eng[ieng]:
            need = {}
            for d in o.deps:
                if hw.get(d.sem, 0) < d.tick and need.get(d.sem, 0) < d.tick:
                    need[d.sem] = d.tick
            for s, t in need.items():
                eng_obj.wait_ge(sems[s], t)
                hw[s] = t
            inst = o.fn(eng_obj)
            if o.need_inc:
                assert inst is not None, ("inc on None inst", o.dbg)
                inst.then_inc(sems[o.sem], 16 if o.is_dma else 1)


def _cols(vec):
    return np.ascontiguousarray(np.asarray(vec, np.float32).reshape(-1, P).T)


def _host_constants():
    c = {}
    t = np.arange(T, dtype=np.float32)
    idx = np.linspace(0.0, 1.0, 64).astype(np.float32)
    freqs = (10.0 ** idx)[None, :].astype(np.float32) * np.arange(NH, dtype=np.float32)[:, None]
    angles = (t[:, None, None] * freqs[None]).astype(np.float32)
    c["cos_fm"] = np.ascontiguousarray(np.cos(angles).astype(np.float32).reshape(T, D).T)
    c["iota_pl"] = np.ascontiguousarray(np.broadcast_to(t, (P, T)))
    c["ones_col"] = np.ones((P, 1), np.float32)
    c["ones_row"] = np.ones((1, P), np.float32)
    c["ident"] = np.eye(P, dtype=np.float32)
    Es = np.zeros((DC, P, NH), np.float32)
    for ch in range(DC):
        Es[ch, 0:64, 2 * ch] = 1.0
        Es[ch, 64:128, 2 * ch + 1] = 1.0
    c["esum"] = np.ascontiguousarray(Es.transpose(1, 0, 2).reshape(P, DC * NH))
    Ee = np.zeros((NH, DC, P), np.float32)
    for ch in range(DC):
        Ee[2 * ch, ch, 0:64] = 1.0
        Ee[2 * ch + 1, ch, 64:128] = 1.0
    c["e16"] = np.ascontiguousarray(Ee.reshape(NH, DC * P))
    m = (np.arange(P)[:, None] <= np.arange(P)[None, :]).astype(np.float32)
    c["mask4"] = np.ascontiguousarray(np.tile(m, (1, 4)))
    return c


def _prep_shared(inp):
    g = {}
    A = lambda k: np.asarray(inp[k], np.float32)
    tr = lambda a: np.ascontiguousarray(np.asarray(a, np.float32).T)
    g["qk_wT"] = tr(A("L0_attn_a_qk_w"))
    g["v_wT"] = tr(A("L0_attn_a_v_w"))
    g["g_wT"] = tr(A("L0_attn_a_g_w"))
    g["o_wT"] = tr(A("L0_attn_a_o_w"))
    g["gate_wT"] = tr(A("L0_attn_a_gate_w"))
    g["up_wT"] = tr(A("L0_attn_a_up_w"))
    g["down_wT"] = tr(A("L0_attn_a_down_w"))
    g["ln1_w_c"] = _cols(A("L0_attn_a_ln1_w"))
    g["ln1_b_c"] = _cols(A("L0_attn_a_ln1_b"))
    g["conv_w_c"] = np.ascontiguousarray(
        A("L0_attn_a_conv_w").reshape(D, 4).reshape(DC, P, 4).transpose(1, 0, 2).reshape(P, DC * 4))
    g["conv_b_c"] = _cols(A("L0_attn_a_conv_b"))
    g["g_b_c"] = _cols(A("L0_attn_a_g_b"))
    g["onorm_w_c"] = _cols(A("L0_attn_a_onorm_w"))
    g["onorm_b_c"] = _cols(A("L0_attn_a_onorm_b"))
    g["gamma1_c"] = _cols(A("L0_attn_a_gamma1"))
    g["gamma2_c"] = _cols(A("L0_attn_a_gamma2"))
    g["ln2_w_c"] = _cols(A("L0_attn_a_ln2_w"))
    g["ln2_b_c"] = _cols(A("L0_attn_a_ln2_b"))
    g["gamma_attn"] = np.ascontiguousarray(A("L0_attn_a_gamma_attn").reshape(NH, 1))
    for nm in ("Wq", "Wk", "Wv", "Wo"):
        g[nm + "T"] = tr(A("L0_attn_b_" + nm))
    g["bq_c"] = _cols(A("L0_attn_b_bq"))
    g["bk_c"] = _cols(A("L0_attn_b_bk"))
    g["bo_c"] = _cols(A("L0_attn_b_bo"))
    g["bv_row"] = np.ascontiguousarray(A("L0_attn_b_bv").reshape(1, D))
    g["bk_row"] = np.ascontiguousarray(A("L0_attn_b_bk").reshape(1, D))
    g["WexpT"] = tr(A("L0_ffn_W_exp"))
    g["Wcon"] = np.ascontiguousarray(A("L0_ffn_W_con"))
    g["b_exp_c"] = _cols(A("L0_ffn_b_exp"))
    g["up_bias_c"] = _cols(A("L0_ffn_up_bias"))
    g["b_con_c"] = _cols(A("L0_ffn_b_con"))
    g["down_bias_c"] = _cols(A("L0_ffn_down_bias"))
    for nm, key in (("bup", "L0_ffn_up_weights"), ("bdn", "L0_ffn_down_weights")):
        w = A(key)
        rows = np.stack([w[:, :, 0, 0], w[:, :, 0, 1], w[:, :, 1, 0], w[:, :, 1, 1]], 1)
        g[nm + "_rows"] = np.ascontiguousarray(rows.reshape(12, 4 * 2048))
        colsf = rows.reshape(12, 4, NH, P).transpose(0, 1, 3, 2)
        g[nm + "_cols"] = np.ascontiguousarray(
            colsf.reshape(12 * 4, P, NH).transpose(1, 0, 2).reshape(P, 12 * 4 * NH))
    g.update(_host_constants())
    return g


_IN_SHAPES = {
    "qk_wT": [D, D], "v_wT": [D, D], "g_wT": [D, D], "o_wT": [D, D],
    "gate_wT": [D, FF], "up_wT": [D, FF], "down_wT": [FF, D],
    "WqT": [D, D], "WkT": [D, D], "WvT": [D, D], "WoT": [D, D],
    "WexpT": [FF, D], "Wcon": [FF, D],
    "cos_fm": [D, T], "iota_pl": [P, T],
    "ln1_w_c": [P, DC], "ln1_b_c": [P, DC], "conv_w_c": [P, DC * 4],
    "conv_b_c": [P, DC], "g_b_c": [P, DC], "onorm_w_c": [P, DC],
    "onorm_b_c": [P, DC], "gamma1_c": [P, DC], "gamma2_c": [P, DC],
    "ln2_w_c": [P, DC], "ln2_b_c": [P, DC], "gamma_attn": [NH, 1],
    "bq_c": [P, DC], "bk_c": [P, DC], "bo_c": [P, DC],
    "bv_row": [1, D], "bk_row": [1, D],
    "b_exp_c": [P, FC], "up_bias_c": [P, FC], "b_con_c": [P, DC],
    "down_bias_c": [P, DC],
    "bup_rows": [12, 4 * 2048], "bdn_rows": [12, 4 * 2048],
    "bup_cols": [P, 12 * 4 * NH], "bdn_cols": [P, 12 * 4 * NH],
    "ones_col": [P, 1], "ones_row": [1, P], "ident": [P, P],
    "esum": [P, DC * NH], "e16": [NH, DC * P], "mask4": [P, 4 * P],
}


class Reg:
    __slots__ = ("ar", "off", "w", "tok", "name")

    def __init__(self, ar, off, w, name):
        self.ar = ar
        self.off = off
        self.w = w
        self.tok = object()
        self.name = name

    @property
    def f(self):
        return self.ar.t[:, self.off:self.off + self.w]

    @property
    def b(self):
        return self.ar.t[:, self.off:self.off + self.w].bitcast(BF16)

    def fr(self, r0, r1, c0, c1):
        return self.ar.t[r0:r1, self.off + c0:self.off + c1]


class Arena:
    def __init__(self, nc, stack, width):
        self.th = stack.enter_context(nc.sbuf_tensor("ARENA", [P, width], F32))
        self.t = self.th[:]
        self.tb = self.th[:].bitcast(BF16)
        self.width = width
        self.hi = 0
        self.frees = []

    def alloc(self, name, wf32):
        wf32 = (wf32 + 3) & ~3
        for i, (o, w) in enumerate(self.frees):
            if w >= wf32:
                self.frees.pop(i)
                if w > wf32:
                    self.frees.append((o + wf32, w - wf32))
                return Reg(self, o, wf32, name)
        o = self.hi
        self.hi += wf32
        assert self.hi <= self.width, f"arena overflow at {name}: {self.hi}"
        return Reg(self, o, wf32, name)

    def free(self, *regs):
        for r in regs:
            self.frees.append((r.off, r.w))
        self.frees.sort()
        out = []
        for o, w in self.frees:
            if out and out[-1][0] + out[-1][1] == o:
                out[-1] = (out[-1][0], out[-1][1] + w)
            else:
                out.append((o, w))
        if out and out[-1][0] + out[-1][1] == self.hi:
            self.hi = out.pop()[0]
        self.frees = [tuple(x) for x in out]


class Prog:
    def __init__(self, stop=None, dbg=None):
        self.stop = stop
        self.dbg_name = dbg
        self.nc = bass.Bass("TRN2", target_bir_lowering=False)
        self.S = Sched()
        self.stack = ExitStack()

    def build(self):
        nc, S = self.nc, self.S
        hb = {}
        hb["x_fm"] = nc.dram_tensor("x_fm", [D, T], F32, kind="ExternalInput").ap()
        for nm, sh in _IN_SHAPES.items():
            hb[nm] = nc.dram_tensor(nm, sh, F32, kind="ExternalInput").ap()
        hb["y"] = nc.dram_tensor("y", [D, T], F32, kind="ExternalOutput").ap()
        hb["xmid"] = nc.dram_tensor("s_xmid", [D, T], F32, kind="ExternalOutput").ap()
        hb["x1"] = nc.dram_tensor("s_x1", [D, T], F32, kind="ExternalOutput").ap()
        hb["x2"] = nc.dram_tensor("s_x2", [D, T], F32, kind="ExternalOutput").ap()
        hb["x3"] = nc.dram_tensor("s_x3", [D, T], F32, kind="ExternalOutput").ap()
        hb["W1"] = nc.dram_tensor("s_W1", [D, FF], BF16, kind="ExternalOutput").ap()
        hb["W2"] = nc.dram_tensor("s_W2", [FF, D], BF16, kind="ExternalOutput").ap()
        hb["hbuf"] = nc.dram_tensor("s_hbuf", [P, FC * T], BF16, kind="ExternalOutput").ap()
        self.hb = hb
        self.htok = {k: object() for k in ("xmid", "x1", "x2", "x3", "y",
                                           "W1", "W2", "hbuf", "brow")}
        self.stok = {k: [object() for _ in range(NT)]
                     for k in ("xmid", "x1", "x2", "x3")}
        self.hbtok = [[object() for _ in range(FC)] for _ in range(NT2)]

        self.A = Arena(nc, self.stack, ARENA_W)
        self.pbs = [self.stack.enter_context(nc.psum_tensor(f"pb{i}", [P, 512], F32))
                    for i in range(7)]
        self.pb = [t[:] for t in self.pbs]
        self.pbt = [object() for _ in range(7)]
        self.pbhs = self.stack.enter_context(nc.psum_tensor("pbh", [P, 1024], BF16))
        self.pbh = self.pbhs[:]
        self.pbht = object()
        self.pbh2t = [object(), object()]

        self.load_consts()
        S.barrier()
        st = self.stop
        self.p0_compose()
        S.barrier()
        self.p1_mixer()
        S.barrier()
        if st != "p1":
            self.p2_mlp()
            S.barrier()
        if st not in ("p1", "p2"):
            self.p3_ffn("x1", "x2", "f1")
            S.barrier()
        if st not in ("p1", "p2", "p3"):
            self.p4_attnb()
            S.barrier()
        if st not in ("p1", "p2", "p3", "p4"):
            self.p3_ffn("x3", "y", "f2")
            S.barrier()
        self.finish()

    def load_consts(self):
        S, hb, A = self.S, self.hb, self.A
        cc = {}
        self.cc = cc

        def new(name, w):
            cc[name] = A.alloc("c_" + name, w)
            return cc[name]

        def ld(name, src, w, used=None):
            r = new(name, w)
            S.dma("sdma", r.f[:, 0:(used or w)], src, w=[r.tok], dbg="ld_" + name)
            return r

        for nm, w in (("ln1_w", DC), ("ln1_b", DC), ("conv_b", DC), ("g_b", DC),
                      ("onorm_w", DC), ("onorm_b", DC), ("gamma1", DC),
                      ("gamma2", DC), ("ln2_w", DC), ("ln2_b", DC),
                      ("bq", DC), ("bk", DC), ("bo", DC), ("b_con", DC),
                      ("down_bias", DC)):
            ld(nm, hb[nm + "_c"], w)
        ld("conv_w", hb["conv_w_c"], DC * 4)
        ld("b_exp", hb["b_exp_c"], FC)
        ld("up_bias", hb["up_bias_c"], FC)
        ld("esum", hb["esum"], DC * NH)
        ld("mask4", hb["mask4"], 4 * P)
        ld("ident", hb["ident"], P)
        ld("iota", hb["iota_pl"], T)
        ld("ones_col", hb["ones_col"], 4, used=1)
        ld("bup_cols", hb["bup_cols"], 12 * 4 * NH)
        ld("bdn_cols", hb["bdn_cols"], 12 * 4 * NH)
        oro = new("ones_row", 128)
        S.dma("sdma", oro.fr(0, 1, 0, P), hb["ones_row"], w=[oro.tok], dbg="ld_ones_row")
        e16 = new("e16", DC * P)
        S.dma("sdma", e16.fr(0, NH, 0, DC * P), hb["e16"], w=[e16.tok], dbg="ld_e16")
        gam = new("gam", 4)
        S.dma("sdma", gam.fr(0, NH, 0, 1), hb["gamma_attn"], w=[gam.tok], dbg="ld_gam")
        idb = new("identb", P // 2)
        self.rows = new("rows", 2048)
        cwf = new("conv_wf", DC * 4)
        cbf = new("conv_bf", DC)
        ecor = new("edge", DC * 3 + 4)
        g1p = new("g1p", DC)
        C1 = new("C1", DC)
        bias2 = new("bias2", DC)
        lng = new("lng", DC)
        csum = new("csum", DC)
        bias1 = new("bias1", FC)

        hlp = new("hlp", 4)
        S.op("dve", lambda v: v.memset(hlp.f[:, 0:1], 1e-5), w=[hlp.tok], dbg="eps")
        S.op("dve", lambda v: v.memset(hlp.f[:, 1:2], -1.0), w=[hlp.tok], dbg="neg1")
        cw = cc["conv_w"]
        S.op("dve", lambda v: v.tensor_copy(out=idb.b[:, 0:P], in_=cc["ident"].f),
             r=[cc["ident"].tok], w=[idb.tok], dbg="identb")
        for c in range(DC):
            for t_ in range(3):
                S.op("dve", lambda v, c=c, t_=t_: v.tensor_reduce(
                    out=ecor.f[:, c * 3 + t_:c * 3 + t_ + 1],
                    in_=cw.f[:, 4 * c:4 * c + 3 - t_], axis=AX.X, op=ALU.add),
                    r=[cw.tok], w=[ecor.tok], dbg="edges")
        for c in range(DC):
            S.op("dve", lambda v, c=c: v.tensor_scalar(
                out=ecor.f[:, c * 3:c * 3 + 3], in0=ecor.f[:, c * 3:c * 3 + 3],
                scalar1=cc["ln1_b"].f[:, c:c + 1], scalar2=None, op0=ALU.mult),
                r=[ecor.tok, cc["ln1_b"].tok], w=[ecor.tok], dbg="edgemul")
        for c in range(DC):
            S.op("dve", lambda v, c=c: v.tensor_reduce(
                out=csum.f[:, c:c + 1], in_=cw.f[:, 4 * c:4 * c + 4], axis=AX.X, op=ALU.add),
                r=[cw.tok], w=[csum.tok], dbg="csum")
        for c in range(DC):
            S.op("dve", lambda v, c=c: v.scalar_tensor_tensor(
                out=cbf.f[:, c:c + 1], in0=csum.f[:, c:c + 1],
                scalar=cc["ln1_b"].f[:, c:c + 1], in1=cc["conv_b"].f[:, c:c + 1],
                op0=ALU.mult, op1=ALU.add),
                r=[csum.tok, cc["ln1_b"].tok, cc["conv_b"].tok], w=[cbf.tok], dbg="cbf")
        for c in range(DC):
            S.op("dve", lambda v, c=c: v.tensor_scalar(
                out=cwf.f[:, 4 * c:4 * c + 4], in0=cw.f[:, 4 * c:4 * c + 4],
                scalar1=cc["ln1_w"].f[:, c:c + 1], scalar2=None, op0=ALU.mult),
                r=[cw.tok, cc["ln1_w"].tok], w=[cwf.tok], dbg="cwf")
        S.op("dve", lambda v: v.tensor_tensor(out=C1.f, in0=cc["gamma1"].f,
             in1=cc["onorm_b"].f, op=ALU.mult),
             r=[cc["gamma1"].tok, cc["onorm_b"].tok], w=[C1.tok], dbg="C1")
        S.op("dve", lambda v: v.tensor_tensor(out=g1p.f, in0=cc["gamma1"].f,
             in1=cc["onorm_w"].f, op=ALU.mult),
             r=[cc["gamma1"].tok, cc["onorm_w"].tok], w=[g1p.tok], dbg="g1p")
        S.op("dve", lambda v: v.tensor_tensor(out=bias2.f, in0=cc["b_con"].f,
             in1=cc["down_bias"].f, op=ALU.add),
             r=[cc["b_con"].tok, cc["down_bias"].tok], w=[bias2.tok], dbg="bias2")
        S.op("act", lambda s: s.activation(out=gam.fr(0, NH, 0, 1), in_=gam.fr(0, NH, 0, 1),
             func=AF.Ln), r=[gam.tok], w=[gam.tok], dbg="lngam")
        for c in range(DC):
            S.op("pe", lambda t, c=c: t.matmul(self.pb[6][:, 0:1],
                 e16.fr(0, NH, c * P, (c + 1) * P), gam.fr(0, NH, 0, 1),
                 start=True, stop=True), r=[e16.tok, gam.tok], w=[self.pbt[6]], dbg="lngx")
            S.op("dve", lambda v, c=c: v.tensor_copy(out=lng.f[:, c:c + 1],
                 in_=self.pb[6][:, 0:1]), r=[self.pbt[6]], w=[lng.tok], dbg="lngc")

    def bcast(self, row_ap, bank, w, extra_r, dbg):
        oro = self.cc["ones_row"]
        self.S.op("pe", lambda t, r_=row_ap, b=bank, w_=w: t.matmul(
            self.pb[b][:, 0:w_], oro.fr(0, 1, 0, P), r_, start=True, stop=True),
            r=[oro.tok] + list(extra_r), w=[self.pbt[bank]], dbg=dbg)

    def ln_stats(self, xt_ap, xtok, sqr, w, pref, sq_chunked=None):
        S, R = self.S, self.rows
        oc_ = self.cc["ones_col"]
        for c in range(DC):
            S.op("pe", lambda t, c=c: t.matmul(self.pb[0][0:1, 0:w], oc_.f[:, 0:1],
                 xt_ap[:, c * w:(c + 1) * w], start=(c == 0), stop=(c == DC - 1)),
                 r=[xtok, oc_.tok], w=[self.pbt[0]], dbg=pref + "sum")
        dbuf = sqr.w >= 2 * w
        sqt = [object(), object()]
        for c in range(DC):
            off = (c % 2) * w if dbuf else 0
            tk_ = sqt[c % 2] if dbuf else sqr.tok
            S.op("act", lambda s, c=c, off=off: s.activation(out=sqr.f[:, off:off + w],
                 in_=xt_ap[:, c * w:(c + 1) * w], func=AF.Square),
                 r=[xtok], w=[tk_], dbg=pref + "sq")
            S.op("pe", lambda t, c=c, off=off: t.matmul(self.pb[1][0:1, 0:w], oc_.f[:, 0:1],
                 sqr.f[:, off:off + w], start=(c == 0), stop=(c == DC - 1)),
                 r=[tk_, oc_.tok], w=[self.pbt[1]], dbg=pref + "sumsq")
        rt = R.tok
        S.op("dve", lambda v: v.tensor_scalar(out=R.fr(0, 1, 0, w), in0=self.pb[0][0:1, 0:w],
             scalar1=1.0 / D, scalar2=None, op0=ALU.mult), r=[self.pbt[0]], w=[rt], dbg=pref + "m")
        S.op("dve", lambda v: v.tensor_scalar(out=R.fr(0, 1, w, 2 * w), in0=self.pb[1][0:1, 0:w],
             scalar1=1.0 / D, scalar2=None, op0=ALU.mult), r=[self.pbt[1]], w=[rt], dbg=pref + "e2")
        S.op("dve", lambda v: v.tensor_tensor(out=R.fr(0, 1, 2 * w, 3 * w), in0=R.fr(0, 1, 0, w),
             in1=R.fr(0, 1, 0, w), op=ALU.mult), r=[rt], w=[rt], dbg=pref + "m2")
        S.op("dve", lambda v: v.tensor_tensor(out=R.fr(0, 1, 2 * w, 3 * w), in0=R.fr(0, 1, w, 2 * w),
             in1=R.fr(0, 1, 2 * w, 3 * w), op=ALU.subtract), r=[rt], w=[rt], dbg=pref + "v")
        ht = self.cc["hlp"]
        S.op("act", lambda s: s.activation(out=R.fr(0, 1, 3 * w, 4 * w), in_=R.fr(0, 1, 2 * w, 3 * w),
             func=AF.Sqrt, bias=ht.fr(0, 1, 0, 1)), r=[rt, ht.tok], w=[rt], dbg=pref + "sd")
        S.op("dve", lambda v: v.reciprocal(out=R.fr(0, 1, 3 * w, 4 * w), in_=R.fr(0, 1, 3 * w, 4 * w)),
             r=[rt], w=[rt], dbg=pref + "r")

    def ln_apply(self, xt_ap, xtok, out_ap, otok, w, pref, tmp=None, scale_col=None):
        S, R = self.S, self.rows
        rt = R.tok
        S.op("dve", lambda v: v.tensor_tensor(out=R.fr(0, 1, w, 2 * w), in0=R.fr(0, 1, 0, w),
             in1=R.fr(0, 1, 3 * w, 4 * w), op=ALU.mult), r=[rt], w=[rt], dbg=pref + "mr")
        self.bcast(R.fr(0, 1, 3 * w, 4 * w), 2, w, [rt], pref + "rpl")
        self.bcast(R.fr(0, 1, w, 2 * w), 3, w, [rt], pref + "mrpl")
        if tmp is not None:
            for c in range(DC):
                S.op("dve", lambda v, c=c: v.tensor_tensor(out=tmp.f[:, 0:w],
                     in0=xt_ap[:, c * w:(c + 1) * w], in1=self.pb[2][:, 0:w], op=ALU.mult),
                     r=[xtok, self.pbt[2]], w=[tmp.tok], dbg=pref + f"a{c}")
                if scale_col is None:
                    S.op("dve", lambda v, c=c: v.tensor_tensor(out=out_ap[:, c * w:(c + 1) * w],
                         in0=tmp.f[:, 0:w], in1=self.pb[3][:, 0:w], op=ALU.subtract),
                         r=[tmp.tok, self.pbt[3]], w=[otok], dbg=pref + f"b{c}")
                else:
                    reg, _ = scale_col
                    S.op("dve", lambda v, c=c: v.tensor_tensor(out=tmp.f[:, 0:w],
                         in0=tmp.f[:, 0:w], in1=self.pb[3][:, 0:w], op=ALU.subtract),
                         r=[tmp.tok, self.pbt[3]], w=[tmp.tok], dbg=pref + f"b{c}")
                    S.op("dve", lambda v, c=c, reg=reg: v.tensor_scalar(
                         out=out_ap[:, c * w:(c + 1) * w], in0=tmp.f[:, 0:w],
                         scalar1=reg.f[:, c:c + 1], scalar2=None, op0=ALU.mult),
                         r=[tmp.tok, reg.tok], w=[otok], dbg=pref + f"s{c}")
            return
        for c in range(DC):
            S.op("dve", lambda v, c=c: v.tensor_tensor(out=out_ap[:, c * w:(c + 1) * w],
                 in0=xt_ap[:, c * w:(c + 1) * w], in1=self.pb[2][:, 0:w], op=ALU.mult),
                 r=[xtok, self.pbt[2]], w=[otok], dbg=pref + f"a{c}")
        for c in range(DC):
            S.op("dve", lambda v, c=c: v.tensor_tensor(out=out_ap[:, c * w:(c + 1) * w],
                 in0=out_ap[:, c * w:(c + 1) * w], in1=self.pb[3][:, 0:w], op=ALU.subtract),
                 r=[otok, self.pbt[3]], w=[otok], dbg=pref + f"b{c}")

    def p0_compose(self):
        S, hb, A, cc = self.S, self.hb, self.A, self.cc
        G = A.alloc("p0_G", FF)
        planes = A.alloc("p0_pl", 4 * 2048)
        m1 = A.alloc("p0_m1", FC * TW)
        m2 = A.alloc("p0_m2", FC * TW)
        wsl = A.alloc("p0_wsl", FC * TW)
        cbb = A.alloc("p0_cbb", FC * TW // 2)
        bm = A.alloc("p0_bm", FC)
        bs = A.alloc("p0_bs", FC)
        pltok = [object() for _ in range(4)]
        m1t = [object() for _ in range(FC)]
        m2t = [object() for _ in range(FC)]
        wslt = [object() for _ in range(FC)]
        cbbt = [object() for _ in range(FC)]
        bmt = [object() for _ in range(FC)]
        bst = [object() for _ in range(FC)]

        def build_G(rows_hb, down):
            for b in range(FC):
                S.op("dve", lambda v, b=b: v.tensor_copy(out=G.f[:, b * P:(b + 1) * P],
                     in_=cc["ident"].f), r=[cc["ident"].tok], w=[G.tok], dbg="Ginit")
            order = range(6, -1, -1) if down else range(7)
            for j in order:
                s = 1 << j
                nblk = 64 // s
                for ci in range(4):
                    S.dma("adma", planes.f[:, ci * 2048:(ci + 1) * 2048],
                          rows_hb[j:j + 1, ci * 2048:(ci + 1) * 2048].partition_broadcast(P).squeeze(1),
                          w=[pltok[ci]], dbg=f"pl{j}_{ci}")
                Gv = G.f.rearrange("p (b n two sg) -> p b n two sg", b=FC, n=nblk, two=2)
                Ga, Gb = Gv[:, :, :, 0, :], Gv[:, :, :, 1, :]
                PL = [planes.f[:, ci * 2048:(ci + 1) * 2048].rearrange(
                    "p (b n sg) -> p b n sg", b=FC, n=nblk) for ci in range(4)]
                iA, iB = (0, 2) if down else (0, 1)
                iC, iD = (1, 3) if down else (2, 3)
                ta = m1.f[:, 0:2048].rearrange("p (b n sg) -> p b n sg", b=FC, n=nblk)
                tb = m1.f[:, 2048:4096].rearrange("p (b n sg) -> p b n sg", b=FC, n=nblk)
                tc_ = m2.f[:, 0:2048].rearrange("p (b n sg) -> p b n sg", b=FC, n=nblk)
                td = m2.f[:, 2048:4096].rearrange("p (b n sg) -> p b n sg", b=FC, n=nblk)
                S.op("dve", lambda v, Ga=Ga, pl=PL[iA], ta=ta: v.tensor_tensor(out=ta, in0=Ga, in1=pl, op=ALU.mult), r=[G.tok, pltok[iA]], w=[m1.tok] + m1t[0:8], dbg=f"G{j}a")
                S.op("pool", lambda g, Gb=Gb, pl=PL[iB], tb=tb: g.tensor_tensor(out=tb, in0=Gb, in1=pl, op=ALU.mult), r=[G.tok, pltok[iB]], w=[m1.tok] + m1t[8:16], dbg=f"G{j}b")
                S.op("dve", lambda v, Ga=Ga, pl=PL[iC], tc_=tc_: v.tensor_tensor(out=tc_, in0=Ga, in1=pl, op=ALU.mult), r=[G.tok, pltok[iC]], w=[m2.tok] + m2t[0:8], dbg=f"G{j}c")
                S.op("pool", lambda g, Gb=Gb, pl=PL[iD], td=td: g.tensor_tensor(out=td, in0=Gb, in1=pl, op=ALU.mult), r=[G.tok, pltok[iD]], w=[m2.tok] + m2t[8:16], dbg=f"G{j}d")
                S.op("dve", lambda v, Ga=Ga, ta=ta, tb=tb: v.tensor_tensor(out=Ga, in0=ta, in1=tb, op=ALU.add), r=[m1.tok], w=[G.tok], dbg=f"G{j}e")
                S.op("pool", lambda g, Gb=Gb, tc_=tc_, td=td: g.tensor_tensor(out=Gb, in0=tc_, in1=td, op=ALU.add), r=[m2.tok], w=[G.tok], dbg=f"G{j}f")

        def c_stage_coeffs(j, cch, up):
            sc = 1 << (j - 7)
            bit = (cch >> (j - 7)) & 1
            cA = cch & ~(1 << (j - 7))
            cB = cA | (1 << (j - 7))
            k0 = ((cch >> (j - 6)) << j) + ((cch & (sc - 1)) << 7)
            col = k0 >> 7
            if up:
                ia, ib = ((0, 1) if bit == 0 else (2, 3))
            else:
                ia, ib = ((0, 2) if bit == 0 else (1, 3))
            cols = cc["bup_cols" if up else "bdn_cols"]
            colA = cols.f[:, (j * 4 + ia) * NH + col:(j * 4 + ia) * NH + col + 1]
            colB = cols.f[:, (j * 4 + ib) * NH + col:(j * 4 + ib) * NH + col + 1]
            return cA, cB, colA, colB, cols.tok

        def c_stages(src, stoks, a, atoks, b, btoks, w, up, pref):
            cur, tcur = src, stoks
            nxt, tnxt = a, atoks
            order = range(7, 12) if up else range(11, 6, -1)
            for j in order:
                for cch in range(FC):
                    cA, cB, colA, colB, ctok = c_stage_coeffs(j, cch, up)
                    S.op("act", lambda s, cur=cur, nxt=nxt, cA=cA, cB=cB, colB=colB, cch=cch:
                         s.activation(out=nxt.f[:, cch * w:(cch + 1) * w],
                         in_=cur.f[:, cB * w:(cB + 1) * w], func=AF.Copy, scale=colB),
                         r=[tcur[cB], ctok], w=[tnxt[cch]],
                         dbg=f"{pref}cs{j}_{cch}a")
                    S.op("dve", lambda v, cur=cur, nxt=nxt, cA=cA, colA=colA, cch=cch:
                         v.scalar_tensor_tensor(out=nxt.f[:, cch * w:(cch + 1) * w],
                         in0=cur.f[:, cA * w:(cA + 1) * w], scalar=colA,
                         in1=nxt.f[:, cch * w:(cch + 1) * w], op0=ALU.mult, op1=ALU.add),
                         r=[tcur[cA], ctok, tnxt[cch]], w=[tnxt[cch]], dbg=f"{pref}cs{j}_{cch}b")
                cur, tcur = nxt, tnxt
                if cur is a:
                    nxt, tnxt = b, btoks
                else:
                    nxt, tnxt = a, atoks
            return cur, tcur

        build_G(hb["bup_rows"], False)
        for sl in range(4):
            S.dma("sdma", wsl.f.rearrange("p (b t) -> p b t", b=FC),
                  hb["WexpT"].rearrange("(b p) i -> p b i", p=P)[:, :, sl * TW:(sl + 1) * TW],
                  w=wslt, dbg=f"wexp{sl}")
            for b in range(FC):
                bank = 5 + (b % 2)
                S.op("pe", lambda t, b=b, bank=bank: t.matmul(self.pb[bank][:, 0:TW],
                     G.f[:, b * P:(b + 1) * P], wsl.f[:, b * TW:(b + 1) * TW],
                     start=True, stop=True), r=[G.tok, wslt[b]], w=[self.pbt[bank]], dbg=f"gm{sl}_{b}")
                S.op("dve", lambda v, b=b, bank=bank: v.tensor_copy(out=m1.f[:, b * TW:(b + 1) * TW],
                     in_=self.pb[bank][:, 0:TW]), r=[self.pbt[bank]], w=[m1t[b]], dbg=f"ge{sl}_{b}")
            fin, fint = c_stages(m1, m1t, m2, m2t, m1, m1t, TW, True, f"u{sl}")
            oth, otht = (m2, m2t) if fin is m1 else (m1, m1t)
            for b in range(FC):
                S.op("dve", lambda v, b=b, fin=fin: v.tensor_copy(
                     out=cbb.b[:, b * TW:(b + 1) * TW], in_=fin.f[:, b * TW:(b + 1) * TW]),
                     r=[fint[b]], w=[cbbt[b]], dbg=f"cst{sl}_{b}")
            for b in range(FC):
                for gg in range(2):
                    S.op("pe", lambda t, b=b, gg=gg: t.transpose(
                         self.pbh[:, 0:P],
                         cbb.b[:, b * TW + gg * P:b * TW + (gg + 1) * P],
                         cc["identb"].b[:, 0:P]), r=[cbbt[b], cc["identb"].tok],
                         w=[self.pbht], dbg=f"tr{sl}_{b}_{gg}")
                    S.op("dve", lambda v, b=b, gg=gg, oth=oth: v.tensor_copy(
                         out=oth.b[:, (gg * FC + b) * P:(gg * FC + b + 1) * P],
                         in_=self.pbh[:, 0:P]), r=[self.pbht],
                         w=[otht[(gg * FC + b) // 4]], dbg=f"tc{sl}_{b}_{gg}")
            for gg in range(2):
                S.dma("adma", hb["W1"][sl * TW + gg * P:sl * TW + (gg + 1) * P, :],
                      oth.b[:, gg * FC * P:(gg + 1) * FC * P], r=otht[8 * gg:8 * (gg + 1)],
                      w=[self.htok["W1"]], dbg=f"w1o{sl}_{gg}")
        for b in range(FC):
            bank = 5 + (b % 2)
            S.op("pe", lambda t, b=b, bank=bank: t.matmul(self.pb[bank][:, 0:1], G.f[:, b * P:(b + 1) * P],
                 cc["b_exp"].f[:, b:b + 1], start=True, stop=True),
                 r=[G.tok, cc["b_exp"].tok], w=[self.pbt[bank]], dbg=f"b1m{b}")
            S.op("dve", lambda v, b=b, bank=bank: v.tensor_copy(out=bm.f[:, b:b + 1],
                 in_=self.pb[bank][:, 0:1]), r=[self.pbt[bank]], w=[bmt[b]], dbg=f"b1e{b}")
        cur, tcur = bm, bmt
        nxt, tnxt = bs, bst
        for j in range(7, 12):
            for cch in range(FC):
                cA, cB, colA, colB, ctok = c_stage_coeffs(j, cch, True)
                S.op("act", lambda s, cur=cur, nxt=nxt, cB=cB, colB=colB, cch=cch:
                     s.activation(out=nxt.f[:, cch:cch + 1], in_=cur.f[:, cB:cB + 1],
                     func=AF.Copy, scale=colB), r=[tcur[cB], ctok],
                     w=[tnxt[cch]], dbg=f"b1c{j}_{cch}a")
                S.op("dve", lambda v, cur=cur, nxt=nxt, cA=cA, colA=colA, cch=cch:
                     v.scalar_tensor_tensor(out=nxt.f[:, cch:cch + 1],
                     in0=cur.f[:, cA:cA + 1], scalar=colA, in1=nxt.f[:, cch:cch + 1],
                     op0=ALU.mult, op1=ALU.add), r=[tcur[cA], ctok, tnxt[cch]],
                     w=[tnxt[cch]], dbg=f"b1c{j}_{cch}b")
            cur, tcur, nxt, tnxt = nxt, tnxt, cur, tcur
        S.op("dve", lambda v, cur=cur: v.tensor_tensor(out=cc["bias1"].f, in0=cur.f[:, 0:FC],
             in1=cc["up_bias"].f, op=ALU.add), r=list(tcur) + [cc["up_bias"].tok],
             w=[cc["bias1"].tok], dbg="bias1")

        build_G(hb["bdn_rows"], True)
        for sl in range(4):
            S.dma("sdma", wsl.f.rearrange("p (b t) -> p b t", b=FC),
                  hb["Wcon"].rearrange("(b p) i -> p b i", p=P)[:, :, sl * TW:(sl + 1) * TW],
                  w=wslt, dbg=f"wcon{sl}")
            fin, fint = c_stages(wsl, wslt, m1, m1t, m2, m2t, TW, False, f"d{sl}")
            for b in range(FC):
                bank = 5 + (b % 2)
                S.op("pe", lambda t, b=b, bank=bank, fin=fin: t.matmul(self.pb[bank][:, 0:TW],
                     G.f[:, b * P:(b + 1) * P], fin.f[:, b * TW:(b + 1) * TW],
                     start=True, stop=True), r=[G.tok, fint[b]], w=[self.pbt[bank]], dbg=f"g2m{sl}_{b}")
                S.op("dve", lambda v, b=b, bank=bank: v.tensor_copy(out=cbb.b[:, b * TW:(b + 1) * TW],
                     in_=self.pb[bank][:, 0:TW]), r=[self.pbt[bank]], w=[cbbt[b]], dbg=f"g2e{sl}_{b}")
            S.dma("adma", hb["W2"].rearrange("(b p) i -> p b i", p=P)[:, :, sl * TW:(sl + 1) * TW],
                  cbb.b[:, 0:FC * TW].rearrange("p (b t) -> p b t", b=FC), r=cbbt,
                  w=[self.htok["W2"]], dbg=f"w2o{sl}")
        A.free(G, planes, m1, m2, wsl, cbb, bm, bs)

    def preload_w(self, names, dsts, pref):
        S, hb, A = self.S, self.hb, self.A
        sl = A.alloc(pref + "_slab", 2 * 2048)
        stok = [object(), object()]
        i = 0
        for nm, dst in zip(names, dsts):
            for q in range(4):
                par = i % 2
                dsl = sl.f[:, par * 2048:(par + 1) * 2048]
                S.dma("sdma", dsl.rearrange("p (c o) -> p c o", c=2),
                      hb[nm].rearrange("(c p) o -> p c o", p=P)[:, 2 * q:2 * q + 2, :],
                      w=[stok[par]], dbg=f"{pref}ld{nm}{q}")
                S.op("dve", lambda v, dst=dst, q=q, dsl=dsl: v.tensor_copy(
                     out=dst.b[:, q * 2048:(q + 1) * 2048], in_=dsl),
                     r=[stok[par]], w=[dst.tok], dbg=f"{pref}cs{nm}{q}")
                i += 1
        A.free(sl)

    def p1_mixer(self):
        S, hb, A, cc = self.S, self.hb, self.A, self.cc
        wqk = A.alloc("p1_wqk", DC * D // 2)
        wv = A.alloc("p1_wv", DC * D // 2)
        wg = A.alloc("p1_wg", DC * D // 2)
        wo = A.alloc("p1_wo", DC * D // 2)
        self.preload_w(("qk_wT", "v_wT", "g_wT", "o_wT"), (wqk, wv, wg, wo), "p1")

        xin = [A.alloc(f"p1_x{i}", DC * TW) for i in range(2)]
        xh1 = A.alloc("p1_xh1", DC * TW)
        cbuf = A.alloc("p1_cb", DC * (TW + 4))
        xcb = A.alloc("p1_xcb", DC * TW // 2)
        kpad = A.alloc("p1_k", DC * (TW + 4))
        vbuf = A.alloc("p1_v", DC * TW)
        gbuf = A.alloc("p1_g", DC * TW)
        cosb = A.alloc("p1_cos", DC * TW)
        s1 = A.alloc("p1_s1", DC * TW)
        sqr = A.alloc("p1_sq", 2 * TW)
        ob = A.alloc("p1_ob", DC * TW // 2)
        xmid = [A.alloc(f"p1_xm{i}", DC * TW) for i in range(2)]
        dpl = A.alloc("p1_dpl", 2 * TW)
        carry = A.alloc("p1_carry", DC)
        R = self.rows

        cv = cbuf.f.rearrange("p (c t) -> p c t", c=DC, t=TW + 4)
        kvv = kpad.f.rearrange("p (c t) -> p c t", c=DC, t=TW + 4)

        S.op("dve", lambda v: v.memset(carry.f, 0.0), w=[carry.tok], dbg="carry0")
        S.op("dve", lambda v: v.memset(cv[:, :, 0:3], 0.0), w=[cbuf.tok], dbg="pad0")
        S.op("dve", lambda v: v.memset(kvv[:, :, 0:1], 0.0), w=[kpad.tok], dbg="kpad0")

        for ti in range(NT):
            par = ti % 2
            xt, xmt = xin[par], xmid[par]
            t0 = ti * TW
            S.dma("sdma", xt.f.rearrange("p (c t) -> p c t", c=DC),
                  hb["x_fm"].rearrange("(c p) t -> p c t", p=P)[:, :, t0:t0 + TW],
                  w=[xt.tok], dbg=f"xin{ti}")
            S.dma("adma", cosb.f.rearrange("p (c t) -> p c t", c=DC),
                  hb["cos_fm"].rearrange("(c p) t -> p c t", p=P)[:, :, t0:t0 + TW],
                  w=[cosb.tok], dbg=f"cosin{ti}")
            self.ln_stats(xt.f, xt.tok, sqr, TW, f"l{ti}")
            rt = R.tok
            S.op("dve", lambda v: v.tensor_tensor(out=R.fr(0, 1, 4 * TW, 5 * TW), in0=R.fr(0, 1, 3 * TW, 4 * TW), in1=R.fr(0, 1, 3 * TW, 4 * TW), op=ALU.mult), r=[rt], w=[rt], dbg=f"r1s{ti}")
            S.op("dve", lambda v: v.tensor_tensor(out=R.fr(0, 1, 4 * TW, 5 * TW), in0=R.fr(0, 1, 2 * TW, 3 * TW), in1=R.fr(0, 1, 4 * TW, 5 * TW), op=ALU.mult), r=[rt], w=[rt], dbg=f"v2{ti}")
            S.op("act", lambda s: s.activation(out=R.fr(0, 1, 4 * TW, 5 * TW), in_=R.fr(0, 1, 4 * TW, 5 * TW), func=AF.Sqrt, bias=cc["hlp"].fr(0, 1, 0, 1)), r=[rt, cc["hlp"].tok], w=[rt], dbg=f"sd2{ti}")
            S.op("dve", lambda v: v.reciprocal(out=R.fr(0, 1, 4 * TW, 5 * TW), in_=R.fr(0, 1, 4 * TW, 5 * TW)), r=[rt], w=[rt], dbg=f"r2{ti}")
            S.op("dve", lambda v: v.tensor_tensor(out=R.fr(0, 1, 5 * TW, 6 * TW), in0=R.fr(0, 1, 0, TW), in1=R.fr(0, 1, 3 * TW, 4 * TW), op=ALU.mult), r=[rt], w=[rt], dbg=f"B1{ti}")
            self.bcast(R.fr(0, 1, 3 * TW, 4 * TW), 2, TW, [rt], f"r1p{ti}")
            self.bcast(R.fr(0, 1, 5 * TW, 6 * TW), 3, TW, [rt], f"B1p{ti}")
            self.bcast(R.fr(0, 1, 4 * TW, 5 * TW), 4, TW, [rt], f"r2p{ti}")
            for c in range(DC):
                S.op("dve", lambda v, c=c, xt=xt: v.tensor_tensor(out=xh1.f[:, c * TW:(c + 1) * TW], in0=xt.f[:, c * TW:(c + 1) * TW], in1=self.pb[2][:, 0:TW], op=ALU.mult), r=[xt.tok, self.pbt[2]], w=[xh1.tok], dbg=f"xa{ti}_{c}")
            for c in range(DC):
                S.op("dve", lambda v, c=c: v.tensor_tensor(out=xh1.f[:, c * TW:(c + 1) * TW], in0=xh1.f[:, c * TW:(c + 1) * TW], in1=self.pb[3][:, 0:TW], op=ALU.subtract), r=[xh1.tok, self.pbt[3]], w=[xh1.tok], dbg=f"xb{ti}_{c}")
            for c in range(DC):
                S.op("dve", lambda v, c=c: v.tensor_tensor(out=cv[:, c, 3:3 + TW], in0=xh1.f[:, c * TW:(c + 1) * TW], in1=self.pb[4][:, 0:TW], op=ALU.mult), r=[xh1.tok, self.pbt[4]], w=[cbuf.tok], dbg=f"x2{ti}_{c}")
            for c in range(DC):
                base = c * (TW + 4)
                o_ = s1.f[:, c * TW:(c + 1) * TW]
                S.op("dve", lambda v, base=base, o_=o_, c=c: v.tensor_scalar(out=o_, in0=cbuf.f[:, base:base + TW], scalar1=cc["conv_wf"].f[:, 4 * c:4 * c + 1], scalar2=cc["conv_bf"].f[:, c:c + 1], op0=ALU.mult, op1=ALU.add), r=[cbuf.tok, cc["conv_wf"].tok, cc["conv_bf"].tok], w=[s1.tok], dbg=f"c0_{ti}_{c}")
                for kk in (1, 2):
                    S.op("dve", lambda v, base=base, o_=o_, c=c, kk=kk: v.scalar_tensor_tensor(out=o_, in0=cbuf.f[:, base + kk:base + kk + TW], scalar=cc["conv_wf"].f[:, 4 * c + kk:4 * c + kk + 1], in1=o_, op0=ALU.mult, op1=ALU.add), r=[cbuf.tok, cc["conv_wf"].tok, s1.tok], w=[s1.tok], dbg=f"c{kk}_{ti}_{c}")
                S.op("dve", lambda v, base=base, o_=o_, c=c: v.scalar_tensor_tensor(out=xcb.b[:, c * TW:(c + 1) * TW], in0=cbuf.f[:, base + 3:base + 3 + TW], scalar=cc["conv_wf"].f[:, 4 * c + 3:4 * c + 4], in1=o_, op0=ALU.mult, op1=ALU.add), r=[cbuf.tok, cc["conv_wf"].tok, s1.tok], w=[xcb.tok], dbg=f"c3_{ti}_{c}")
            if ti == 0:
                for c in range(DC):
                    S.op("dve", lambda v, c=c: v.tensor_tensor(out=xcb.b[:, c * TW:c * TW + 3], in0=xcb.b[:, c * TW:c * TW + 3], in1=cc["edge"].f[:, c * 3:c * 3 + 3], op=ALU.subtract), r=[xcb.tok, cc["edge"].tok], w=[xcb.tok], dbg=f"edg{c}")
            if ti < NT - 1:
                S.op("dve", lambda v: v.tensor_copy(out=cv[:, :, 0:3], in_=cv[:, :, TW:TW + 3]), r=[cbuf.tok], w=[cbuf.tok], dbg=f"pcp{ti}")
            for pi, (wres, nm) in enumerate(((wqk, "qk"), (wv, "v"), (wg, "g"))):
                bank = 5 if pi != 2 else 6
                for oc in range(DC):
                    for kc in range(DC):
                        S.op("pe", lambda t, oc=oc, kc=kc, wres=wres, bank=bank: t.matmul(
                            self.pb[bank][:, 0:TW], wres.b[:, kc * D + oc * P:kc * D + (oc + 1) * P],
                            xcb.b[:, kc * TW:(kc + 1) * TW], start=(kc == 0), stop=(kc == DC - 1)),
                            r=[xcb.tok, wres.tok], w=[self.pbt[bank]], dbg=f"{nm}m{ti}_{oc}")
                    if nm == "qk":
                        S.op("dve", lambda v, oc=oc: v.tensor_copy(out=kvv[:, oc, 1:1 + TW], in_=self.pb[5][:, 0:TW]), r=[self.pbt[5]], w=[kpad.tok], dbg=f"qke{ti}_{oc}")
                    elif nm == "v":
                        S.op("dve", lambda v, oc=oc: v.tensor_copy(out=vbuf.f[:, oc * TW:(oc + 1) * TW], in_=self.pb[5][:, 0:TW]), r=[self.pbt[5]], w=[vbuf.tok], dbg=f"ve{ti}_{oc}")
                    else:
                        S.op("act", lambda s, oc=oc: s.activation(out=gbuf.f[:, oc * TW:(oc + 1) * TW], in_=self.pb[6][:, 0:TW], func=AF.Sigmoid, bias=cc["g_b"].f[:, oc:oc + 1]), r=[self.pbt[6], cc["g_b"].tok], w=[gbuf.tok], dbg=f"ge{ti}_{oc}")
            for c in range(DC):
                S.op("act", lambda s, c=c: s.activation(out=sqr.f[:, 0:TW], in_=kvv[:, c, 1:1 + TW], func=AF.Square), r=[kpad.tok], w=[sqr.tok], dbg=f"ks{ti}_{c}")
                S.op("pe", lambda t, c=c: t.matmul(self.pb[0][0:NH, 0:TW], cc["esum"].f[:, c * NH:(c + 1) * NH], sqr.f[:, 0:TW], start=(c == 0), stop=(c == DC - 1)), r=[sqr.tok, cc["esum"].tok], w=[self.pbt[0]], dbg=f"ss{ti}_{c}")
            S.op("act", lambda s: s.activation(out=R.fr(0, NH, 6 * TW, 7 * TW), in_=self.pb[0][0:NH, 0:TW], func=AF.Sqrt), r=[self.pbt[0]], w=[R.tok], dbg=f"rn1{ti}")
            S.op("dve", lambda v: v.tensor_scalar(out=R.fr(0, NH, 6 * TW, 7 * TW), in0=R.fr(0, NH, 6 * TW, 7 * TW), scalar1=1e-12, scalar2=None, op0=ALU.max), r=[R.tok], w=[R.tok], dbg=f"rn2{ti}")
            S.op("dve", lambda v: v.reciprocal(out=R.fr(0, NH, 6 * TW, 7 * TW), in_=R.fr(0, NH, 6 * TW, 7 * TW)), r=[R.tok], w=[R.tok], dbg=f"rn3{ti}")
            for c in range(DC):
                S.op("pe", lambda t, c=c: t.matmul(self.pb[1][:, 0:TW], cc["e16"].fr(0, NH, c * P, (c + 1) * P), R.fr(0, NH, 6 * TW, 7 * TW), start=True, stop=True), r=[R.tok, cc["e16"].tok], w=[self.pbt[1]], dbg=f"rx{ti}_{c}")
                S.op("dve", lambda v, c=c: v.tensor_tensor(out=kvv[:, c, 1:1 + TW], in0=kvv[:, c, 1:1 + TW], in1=self.pb[1][:, 0:TW], op=ALU.mult), r=[kpad.tok, self.pbt[1]], w=[kpad.tok], dbg=f"kn{ti}_{c}")
            S.op("dve", lambda v: v.tensor_tensor(out=s1.f.rearrange("p (c t) -> p c t", c=DC), in0=cosb.f.rearrange("p (c t) -> p c t", c=DC), in1=kvv[:, :, 0:TW], op=ALU.add), r=[cosb.tok, kpad.tok], w=[s1.tok], dbg=f"m1{ti}")
            S.op("pool", lambda g: g.tensor_tensor(out=s1.f, in0=s1.f, in1=vbuf.f, op=ALU.mult), r=[s1.tok, vbuf.tok], w=[s1.tok], dbg=f"m2{ti}")
            S.op("dve", lambda v: v.tensor_tensor(out=s1.f, in0=s1.f, in1=gbuf.f, op=ALU.mult), r=[s1.tok, gbuf.tok], w=[s1.tok], dbg=f"m3{ti}")
            if ti < NT - 1:
                S.op("dve", lambda v: v.tensor_copy(out=kvv[:, :, 0:1], in_=kvv[:, :, TW:TW + 1]), r=[kpad.tok], w=[kpad.tok], dbg=f"kc{ti}")
            cum = vbuf
            for c in range(DC):
                slc = slice(c * TW, (c + 1) * TW)
                S.op("act", lambda s, c=c, t0=t0: s.activation(out=dpl.f[:, 0:TW], in_=cc["iota"].f[:, t0:t0 + TW], func=AF.Exp, scale=cc["lng"].f[:, c:c + 1]), r=[cc["iota"].tok, cc["lng"].tok], w=[dpl.tok], dbg=f"dk{ti}_{c}")
                S.op("dve", lambda v, slc=slc: v.tensor_tensor(out=s1.f[:, slc], in0=s1.f[:, slc], in1=dpl.f[:, 0:TW], op=ALU.mult), r=[s1.tok, dpl.tok], w=[s1.tok], dbg=f"zd{ti}_{c}")
                S.op("dve", lambda v: v.tensor_scalar(out=dpl.f[:, TW:2 * TW], in0=dpl.f[:, 0:TW], scalar1=1e-6, scalar2=None, op0=ALU.add), r=[dpl.tok], w=[dpl.tok], dbg=f"ra{ti}_{c}")
                S.op("dve", lambda v: v.reciprocal(out=dpl.f[:, TW:2 * TW], in_=dpl.f[:, TW:2 * TW]), r=[dpl.tok], w=[dpl.tok], dbg=f"rb{ti}_{c}")
                S.op("dve", lambda v, slc=slc, c=c: v.tensor_tensor_scan(out=cum.f[:, slc], data0=cc["ones_col"].f[:, 0:1].broadcast_to([P, TW]), data1=s1.f[:, slc], initial=carry.f[:, c:c + 1], op0=ALU.mult, op1=ALU.add), r=[s1.tok, carry.tok, cc["ones_col"].tok], w=[cum.tok], dbg=f"sc{ti}_{c}")
                S.op("dve", lambda v, c=c: v.tensor_copy(out=carry.f[:, c:c + 1], in_=cum.f[:, c * TW + TW - 1:c * TW + TW]), r=[cum.tok], w=[carry.tok], dbg=f"cr{ti}_{c}")
                S.op("dve", lambda v, slc=slc: v.tensor_tensor(out=cum.f[:, slc], in0=cum.f[:, slc], in1=dpl.f[:, TW:2 * TW], op=ALU.mult), r=[cum.tok, dpl.tok], w=[cum.tok], dbg=f"me{ti}_{c}")
                S.op("dve", lambda v, slc=slc, c=c: v.tensor_tensor(out=ob.b[:, slc], in0=cum.f[:, slc], in1=kvv[:, c, 1:1 + TW], op=ALU.mult), r=[cum.tok, kpad.tok], w=[ob.tok], dbg=f"oa{ti}_{c}")
            for oc in range(DC):
                for kc in range(DC):
                    S.op("pe", lambda t, oc=oc, kc=kc: t.matmul(self.pb[5][:, 0:TW], wo.b[:, kc * D + oc * P:kc * D + (oc + 1) * P], ob.b[:, kc * TW:(kc + 1) * TW], start=(kc == 0), stop=(kc == DC - 1)), r=[ob.tok, wo.tok], w=[self.pbt[5]], dbg=f"om{ti}_{oc}")
                S.op("dve", lambda v, oc=oc: v.tensor_copy(out=s1.f[:, oc * TW:(oc + 1) * TW], in_=self.pb[5][:, 0:TW]), r=[self.pbt[5]], w=[s1.tok], dbg=f"oe{ti}_{oc}")
            self.ln_stats(s1.f, s1.tok, sqr, TW, f"o{ti}")
            S.op("dve", lambda v: v.tensor_tensor(out=R.fr(0, 1, TW, 2 * TW), in0=R.fr(0, 1, 0, TW), in1=R.fr(0, 1, 3 * TW, 4 * TW), op=ALU.mult), r=[R.tok], w=[R.tok], dbg=f"omr{ti}")
            self.bcast(R.fr(0, 1, 3 * TW, 4 * TW), 2, TW, [R.tok], f"orp{ti}")
            self.bcast(R.fr(0, 1, TW, 2 * TW), 3, TW, [R.tok], f"omp{ti}")
            for c in range(DC):
                slc = slice(c * TW, (c + 1) * TW)
                S.op("dve", lambda v, slc=slc: v.tensor_tensor(out=s1.f[:, slc], in0=s1.f[:, slc], in1=self.pb[2][:, 0:TW], op=ALU.mult), r=[s1.tok, self.pbt[2]], w=[s1.tok], dbg=f"na{ti}_{c}")
            for c in range(DC):
                slc = slice(c * TW, (c + 1) * TW)
                S.op("dve", lambda v, slc=slc: v.tensor_tensor(out=s1.f[:, slc], in0=s1.f[:, slc], in1=self.pb[3][:, 0:TW], op=ALU.subtract), r=[s1.tok, self.pbt[3]], w=[s1.tok], dbg=f"nb{ti}_{c}")
            for c in range(DC):
                slc = slice(c * TW, (c + 1) * TW)
                S.op("dve", lambda v, slc=slc, c=c, xmt=xmt: v.scalar_tensor_tensor(out=xmt.f[:, slc], in0=s1.f[:, slc], scalar=cc["g1p"].f[:, c:c + 1], in1=xh1.f[:, slc], op0=ALU.mult, op1=ALU.add), r=[s1.tok, xh1.tok, cc["g1p"].tok], w=[xmt.tok], dbg=f"nc{ti}_{c}")
            for c in range(DC):
                slc = slice(c * TW, (c + 1) * TW)
                S.op("dve", lambda v, slc=slc, c=c, xmt=xmt: v.tensor_scalar(out=xmt.f[:, slc], in0=xmt.f[:, slc], scalar1=cc["C1"].f[:, c:c + 1], scalar2=None, op0=ALU.add), r=[xmt.tok, cc["C1"].tok], w=[xmt.tok], dbg=f"nd{ti}_{c}")
            S.dma("adma", hb["xmid"].rearrange("(c p) t -> p c t", p=P)[:, :, t0:t0 + TW],
                  xmt.f.rearrange("p (c t) -> p c t", c=DC), r=[xmt.tok],
                  w=[self.stok["xmid"][ti]], dbg=f"xmo{ti}")
        A.free(wqk, wv, wg, wo, *xin, xh1, cbuf, xcb, kpad, vbuf, gbuf, cosb,
               s1, sqr, ob, *xmid, dpl, carry)

    def p2_mlp(self):
        S, hb, A, cc = self.S, self.hb, self.A, self.cc
        W = TW2

        wgate = A.alloc("p2_wg", DC * FF // 2)
        wup = A.alloc("p2_wu", DC * FF // 2)
        biasg = A.alloc("p2_bg", FC)
        biasu = A.alloc("p2_bu", FC)
        lnbb = A.alloc("p2_lnbb", 8)
        S.op("dve", lambda v: v.tensor_copy(out=lnbb.b[:, 0:DC], in_=cc["ln2_b"].f),
             r=[cc["ln2_b"].tok], w=[lnbb.tok], dbg="p2lnbb")
        sl = A.alloc("p2_slab", 2 * FF)
        stok = [object(), object()]
        for wi, (nm, dst, bias) in enumerate((("gate_wT", wgate, biasg), ("up_wT", wup, biasu))):
            for kc in range(DC):
                par = kc % 2
                dsl = sl.f[:, par * FF:(par + 1) * FF]
                S.dma("sdma", dsl, hb[nm][kc * P:(kc + 1) * P, :], w=[stok[par]], dbg=f"p2l{nm}{kc}")
                S.op("dve", lambda v, kc=kc, dsl=dsl, dst=dst: v.tensor_copy(
                     out=dst.b[:, kc * FF:(kc + 1) * FF], in0=dsl) if False else v.tensor_copy(
                     out=dst.b[:, kc * FF:(kc + 1) * FF], in_=dsl),
                     r=[stok[par]], w=[dst.tok], dbg=f"p2f{nm}{kc}")
            for oc in range(FC):
                for kc in range(DC):
                    S.op("pe", lambda t, kc=kc, oc=oc, dst=dst: t.matmul(
                         self.pb[6][:, oc:oc + 1],
                         dst.b[:, kc * FF + oc * P:kc * FF + (oc + 1) * P],
                         lnbb.b[:, kc:kc + 1],
                         start=(kc == 0), stop=(kc == DC - 1)),
                         r=[dst.tok, lnbb.tok], w=[self.pbt[6]], dbg=f"p2b{nm}{kc}_{oc}")
            S.op("dve", lambda v, bias=bias: v.tensor_copy(out=bias.f[:, 0:FC], in_=self.pb[6][:, 0:FC]), r=[self.pbt[6]], w=[bias.tok], dbg=f"p2bc{wi}")
        A.free(sl)
        S.barrier()

        xm = A.alloc("p2_xm", DC * W)
        xhb = A.alloc("p2_xhb", DC * W // 2)
        sqr = A.alloc("p2_sq", 2 * W)
        sg = A.alloc("p2_sg", W)
        hbc = A.alloc("p2_hbc", W)
        R = self.rows

        for ti in range(NT2):
            t0 = ti * W
            S.dma("sdma", xm.f.rearrange("p (c t) -> p c t", c=DC),
                  hb["xmid"].rearrange("(c p) t -> p c t", p=P)[:, :, t0:t0 + W],
                  r=[self.stok["xmid"][2 * ti], self.stok["xmid"][2 * ti + 1]],
                  w=[xm.tok], dbg=f"p2x{ti}")
            self.ln_stats(xm.f, xm.tok, sqr, W, f"p2l{ti}")
            self.ln_apply(xm.f, xm.tok, xhb.b, xhb.tok, W, f"p2a{ti}", tmp=sqr,
                          scale_col=(cc["ln2_w"], None))
            for oc in range(FC):
                for kc in range(DC):
                    S.op("pe", lambda t, oc=oc, kc=kc: t.matmul(self.pb[5][:, 0:W], wgate.b[:, kc * FF + oc * P:kc * FF + (oc + 1) * P], xhb.b[:, kc * W:(kc + 1) * W], start=(kc == 0), stop=(kc == DC - 1)), r=[xhb.tok, wgate.tok], w=[self.pbt[5]], dbg=f"p2g{ti}_{oc}")
                S.op("act", lambda s, oc=oc: s.activation(out=sg.f[:, 0:W], in_=self.pb[5][:, 0:W], func=AF.Silu, bias=biasg.f[:, oc:oc + 1]), r=[self.pbt[5], biasg.tok], w=[sg.tok], dbg=f"p2s{ti}_{oc}")
                for kc in range(DC):
                    S.op("pe", lambda t, oc=oc, kc=kc: t.matmul(self.pb[6][:, 0:W], wup.b[:, kc * FF + oc * P:kc * FF + (oc + 1) * P], xhb.b[:, kc * W:(kc + 1) * W], start=(kc == 0), stop=(kc == DC - 1)), r=[xhb.tok, wup.tok], w=[self.pbt[6]], dbg=f"p2u{ti}_{oc}")
                S.op("dve", lambda v, oc=oc: v.scalar_tensor_tensor(out=hbc.b[:, 0:W], in0=self.pb[6][:, 0:W], scalar=biasu.f[:, oc:oc + 1], in1=sg.f[:, 0:W], op0=ALU.add, op1=ALU.mult), r=[self.pbt[6], biasu.tok, sg.tok], w=[hbc.tok], dbg=f"p2h{ti}_{oc}")
                S.dma("adma", hb["hbuf"][:, oc * T + t0:oc * T + t0 + W], hbc.b[:, 0:W], r=[hbc.tok], w=[self.hbtok[ti][oc]], dbg=f"p2o{ti}_{oc}")
        A.free(wgate, wup, biasg, biasu, lnbb, xm, xhb, sqr, sg, hbc)
        S.barrier()

        wdn = A.alloc("p2_wdn", FC * D // 2)
        self.preload_w2big("down_wT", wdn, "p2d")
        hbt = A.alloc("p2_hbt", FC * W // 2)
        xm2 = A.alloc("p2_xm2", DC * W)
        xo = A.alloc("p2_xo", DC * W)
        x1t = A.alloc("p2_x1t", DC * W)
        for ti in range(NT2):
            t0 = ti * W
            for hc in range(FC):
                S.dma("sdma", hbt.b[:, hc * W:(hc + 1) * W],
                      hb["hbuf"][:, hc * T + t0:hc * T + t0 + W],
                      r=[self.hbtok[ti][hc]], w=[hbt.tok], dbg=f"p2hi{ti}_{hc}")
            S.dma("sdma", xm2.f.rearrange("p (c t) -> p c t", c=DC),
                  hb["xmid"].rearrange("(c p) t -> p c t", p=P)[:, :, t0:t0 + W],
                  r=[self.stok["xmid"][2 * ti], self.stok["xmid"][2 * ti + 1]],
                  w=[xm2.tok], dbg=f"p2x2{ti}")
            S.dma("adma", xo.f.rearrange("p (c t) -> p c t", c=DC),
                  hb["x_fm"].rearrange("(c p) t -> p c t", p=P)[:, :, t0:t0 + W],
                  w=[xo.tok], dbg=f"p2xo{ti}")
            for c in range(DC):
                S.op("pool", lambda g, c=c: g.tensor_tensor(out=x1t.f[:, c * W:(c + 1) * W], in0=xo.f[:, c * W:(c + 1) * W], in1=xm2.f[:, c * W:(c + 1) * W], op=ALU.add), r=[xo.tok, xm2.tok], w=[x1t.tok], dbg=f"p2p{ti}_{c}")
            for dc_ in range(DC):
                bank = 5
                for kc in range(FC):
                    S.op("pe", lambda t, dc_=dc_, kc=kc, bank=bank: t.matmul(self.pb[bank][:, 0:W], wdn.b[:, kc * D + dc_ * P:kc * D + (dc_ + 1) * P], hbt.b[:, kc * W:(kc + 1) * W], start=(kc == 0), stop=(kc == FC - 1)), r=[hbt.tok, wdn.tok], w=[self.pbt[bank]], dbg=f"p2d{ti}_{dc_}")
                S.op("dve", lambda v, dc_=dc_, bank=bank: v.scalar_tensor_tensor(out=x1t.f[:, dc_ * W:(dc_ + 1) * W], in0=self.pb[bank][:, 0:W], scalar=cc["gamma2"].f[:, dc_:dc_ + 1], in1=x1t.f[:, dc_ * W:(dc_ + 1) * W], op0=ALU.mult, op1=ALU.add), r=[self.pbt[bank], cc["gamma2"].tok, x1t.tok], w=[x1t.tok], dbg=f"p2e{ti}_{dc_}")
            S.dma("sdma", hb["x1"].rearrange("(c p) t -> p c t", p=P)[:, :, t0:t0 + W],
                  x1t.f.rearrange("p (c t) -> p c t", c=DC), r=[x1t.tok],
                  w=[self.stok["x1"][2 * ti], self.stok["x1"][2 * ti + 1]], dbg=f"p2y{ti}")
        A.free(wdn, hbt, xm2, xo, x1t)

    def preload_w2big(self, nm, dst, pref):
        S, hb, A = self.S, self.hb, self.A
        sl = A.alloc(pref + "_slab", 2 * 2048)
        stok = [object(), object()]
        for q in range(FC // 2):
            par = q % 2
            dsl = sl.f[:, par * 2048:(par + 1) * 2048]
            S.dma("sdma", dsl.rearrange("p (c o) -> p c o", c=2),
                  hb[nm].rearrange("(c p) o -> p c o", p=P)[:, 2 * q:2 * q + 2, :],
                  w=[stok[par]], dbg=f"{pref}l{q}")
            S.op("dve", lambda v, q=q, dsl=dsl: v.tensor_copy(
                 out=dst.b[:, q * 2048:(q + 1) * 2048], in_=dsl), r=[stok[par]],
                 w=[dst.tok], dbg=f"{pref}c{q}")
        A.free(sl)

    def p3_ffn(self, src, dst, tag):
        S, hb, A, cc = self.S, self.hb, self.A, self.cc
        W = TW2
        stok = self.htok.get(src)
        w1 = A.alloc(tag + "_w1", DC * FF // 2)
        S.dma("sdma", w1.b[:, 0:DC * FF].rearrange("p (c o) -> p c o", c=DC),
              hb["W1"].rearrange("(c p) o -> p c o", p=P), r=[self.htok["W1"]],
              w=[w1.tok], dbg=tag + "lw1")
        xt = [A.alloc(f"{tag}_x{i}", DC * W) for i in range(2)]
        xhb = A.alloc(tag + "_xhb", DC * W // 2)
        sqr = A.alloc(tag + "_sq", 2 * W)
        hgc = A.alloc(tag + "_hgc", W)
        for ti in range(NT2):
            x_ = xt[ti % 2]
            t0 = ti * W
            rd = ([self.stok[src][2 * ti], self.stok[src][2 * ti + 1]]
                  if src in self.stok else [])
            S.dma("sdma", x_.f.rearrange("p (c t) -> p c t", c=DC),
                  hb[src].rearrange("(c p) t -> p c t", p=P)[:, :, t0:t0 + W],
                  r=rd, w=[x_.tok], dbg=f"{tag}x{ti}")
            self.ln_stats(x_.f, x_.tok, sqr, W, f"{tag}l{ti}")
            self.ln_apply(x_.f, x_.tok, xhb.b, xhb.tok, W, f"{tag}a{ti}", tmp=sqr)
            for oc in range(FC):
                bank = 5 if oc % 2 == 0 else 6
                for kc in range(DC):
                    S.op("pe", lambda t, oc=oc, kc=kc, bank=bank: t.matmul(self.pb[bank][:, 0:W], w1.b[:, kc * FF + oc * P:kc * FF + (oc + 1) * P], xhb.b[:, kc * W:(kc + 1) * W], start=(kc == 0), stop=(kc == DC - 1)), r=[xhb.tok, w1.tok], w=[self.pbt[bank]], dbg=f"{tag}m{ti}_{oc}")
                S.op("act", lambda s, oc=oc, bank=bank: s.activation(out=hgc.b[:, 0:W], in_=self.pb[bank][:, 0:W], func=AF.Gelu, bias=cc["bias1"].f[:, oc:oc + 1]), r=[self.pbt[bank], cc["bias1"].tok], w=[hgc.tok], dbg=f"{tag}g{ti}_{oc}")
                S.dma("adma", hb["hbuf"][:, oc * T + t0:oc * T + t0 + W], hgc.b[:, 0:W], r=[hgc.tok], w=[self.hbtok[ti][oc]], dbg=f"{tag}ho{ti}_{oc}")
        A.free(w1, xhb, sqr, hgc)
        S.barrier()
        w2 = A.alloc(tag + "_w2", FC * D // 2)
        S.dma("sdma", w2.b[:, 0:FC * D].rearrange("p (c o) -> p c o", c=FC),
              hb["W2"].rearrange("(c p) o -> p c o", p=P), r=[self.htok["W2"]],
              w=[w2.tok], dbg=tag + "lw2")
        hgt = A.alloc(tag + "_hgt", FC * W // 2)
        yt = A.alloc(tag + "_yt", DC * W)
        for ti in range(NT2):
            x_ = xt[ti % 2]
            t0 = ti * W
            rd = ([self.stok[src][2 * ti], self.stok[src][2 * ti + 1]]
                  if src in self.stok else [])
            for hc in range(FC):
                S.dma("sdma", hgt.b[:, hc * W:(hc + 1) * W],
                      hb["hbuf"][:, hc * T + t0:hc * T + t0 + W],
                      r=[self.hbtok[ti][hc]], w=[hgt.tok], dbg=f"{tag}hi{ti}_{hc}")
            S.dma("adma", x_.f.rearrange("p (c t) -> p c t", c=DC),
                  hb[src].rearrange("(c p) t -> p c t", p=P)[:, :, t0:t0 + W],
                  r=rd, w=[x_.tok], dbg=f"{tag}x2{ti}")
            for dc_ in range(DC):
                bank = 5 if dc_ % 2 == 0 else 6
                for kc in range(FC):
                    S.op("pe", lambda t, dc_=dc_, kc=kc, bank=bank: t.matmul(self.pb[bank][:, 0:W], w2.b[:, kc * D + dc_ * P:kc * D + (dc_ + 1) * P], hgt.b[:, kc * W:(kc + 1) * W], start=(kc == 0), stop=(kc == FC - 1)), r=[hgt.tok, w2.tok], w=[self.pbt[bank]], dbg=f"{tag}n{ti}_{dc_}")
                S.op("dve", lambda v, dc_=dc_, bank=bank, x_=x_: v.scalar_tensor_tensor(out=yt.f[:, dc_ * W:(dc_ + 1) * W], in0=self.pb[bank][:, 0:W], scalar=cc["bias2"].f[:, dc_:dc_ + 1], in1=x_.f[:, dc_ * W:(dc_ + 1) * W], op0=ALU.add, op1=ALU.add), r=[self.pbt[bank], cc["bias2"].tok, x_.tok], w=[yt.tok], dbg=f"{tag}e{ti}_{dc_}")
            wlist = ([self.stok[dst][2 * ti], self.stok[dst][2 * ti + 1]]
                     if dst in self.stok else [self.htok[dst]])
            S.dma("sdma", hb[dst].rearrange("(c p) t -> p c t", p=P)[:, :, t0:t0 + W],
                  yt.f.rearrange("p (c t) -> p c t", c=DC), r=[yt.tok],
                  w=wlist, dbg=f"{tag}y{ti}")
        A.free(w2, hgt, yt, *xt)

    def p4_attnb(self):
        S, hb, A, cc = self.S, self.hb, self.A, self.cc
        wq = A.alloc("p4_wq", DC * D // 2)
        wk = A.alloc("p4_wk", DC * D // 2)
        wv_ = A.alloc("p4_wv", DC * D // 2)
        wo = A.alloc("p4_wo", DC * D // 2)
        self.preload_w(("WqT", "WkT", "WvT", "WoT"), (wq, wk, wv_, wo), "p4")
        bvr = A.alloc("p4_bvr", D)
        S.dma("sdma", bvr.fr(0, 1, 0, D), hb["bv_row"], w=[bvr.tok], dbg="p4bv")
        bkr = A.alloc("p4_bkr", D)
        S.dma("sdma", bkr.fr(0, 1, 0, D), hb["bk_row"], w=[bkr.tok], dbg="p4bk")

        xt = [A.alloc("p4_x0", DC * TW)] * 2
        xhb = A.alloc("p4_xhb", DC * TW // 2)
        sqr = A.alloc("p4_sq", 2 * TW)
        sqf = A.alloc("p4_sqf", DC * TW)
        qf = A.alloc("p4_qf", DC * TW)
        kf = A.alloc("p4_kf", DC * TW)
        qf2 = A.alloc("p4_qf2", DC * TW)
        kf2 = A.alloc("p4_kf2", DC * TW)
        ktok = A.alloc("p4_kt", 2 * D)
        vo = A.alloc("p4_vo", 2 * NH * 66 + 8)
        sT = A.alloc("p4_sT", 4 * P)
        att = A.alloc("p4_att", D)
        atf = A.alloc("p4_atf", DC * TW // 2)
        dcol = A.alloc("p4_dcol", 4)
        Mst = A.alloc("p4_M", NH * 66 + 8)
        yt = A.alloc("p4_yt", DC * TW)
        R = self.rows

        if self.dbg_name == "att":
            self.dbg_att = self.nc.dram_tensor("dbg_att", [P, 2 * D], BF16, kind="ExternalOutput").ap()
        S.op("dve", lambda v: v.memset(Mst.fr(0, 64, 0, NH * 66), 0.0), w=[Mst.tok], dbg="M0")
        S.op("dve", lambda v: v.memset(
            vo.f[:, 0:2 * NH * 66].rearrange("p (c h e) -> p c h e", c=2, h=NH)[:, :, :, 64:66],
            1.0), w=[vo.tok], dbg="vo1")

        def softplus(ap, tok_, wd, pref):
            S.op("act", lambda s: s.activation(out=sqf.f[:, 0:wd], in_=ap, func=AF.Abs), r=[tok_], w=[sqf.tok], dbg=pref + "ab")
            S.op("act", lambda s: s.activation(out=sqf.f[:, 0:wd], in_=sqf.f[:, 0:wd], func=AF.Exp, scale=cc["hlp"].f[:, 1:2]), r=[sqf.tok, cc["hlp"].tok], w=[sqf.tok], dbg=pref + "ex")
            S.op("act", lambda s: s.activation(out=sqf.f[:, 0:wd], in_=sqf.f[:, 0:wd], func=AF.Ln, bias=cc["ones_col"].f[:, 0:1]), r=[sqf.tok, cc["ones_col"].tok], w=[sqf.tok], dbg=pref + "ln")
            S.op("dve", lambda v: v.tensor_scalar(out=ap, in0=ap, scalar1=0.0, scalar2=None, op0=ALU.max), r=[tok_], w=[tok_], dbg=pref + "re")
            S.op("dve", lambda v: v.tensor_tensor(out=ap, in0=ap, in1=sqf.f[:, 0:wd], op=ALU.add), r=[tok_, sqf.tok], w=[tok_], dbg=pref + "ad")

        for ti in range(NT):
            x_ = xt[ti % 2]
            t0 = ti * TW
            S.dma("sdma", x_.f.rearrange("p (c t) -> p c t", c=DC),
                  hb["x2"].rearrange("(c p) t -> p c t", p=P)[:, :, t0:t0 + TW],
                  r=[self.stok["x2"][ti]], w=[x_.tok], dbg=f"p4x{ti}")
            self.ln_stats(x_.f, x_.tok, sqr, TW, f"p4l{ti}")
            self.ln_apply(x_.f, x_.tok, xhb.b, xhb.tok, TW, f"p4a{ti}", tmp=sqr)
            for nm, wres, dst_, bias in (("q", wq, qf, "bq"), ("k", wk, kf, "bk")):
                for oc in range(DC):
                    for kc in range(DC):
                        S.op("pe", lambda t, oc=oc, kc=kc, wres=wres: t.matmul(self.pb[5][:, 0:TW], wres.b[:, kc * D + oc * P:kc * D + (oc + 1) * P], xhb.b[:, kc * TW:(kc + 1) * TW], start=(kc == 0), stop=(kc == DC - 1)), r=[xhb.tok, wres.tok], w=[self.pbt[5]], dbg=f"p4{nm}m{ti}_{oc}")
                    S.op("dve", lambda v, oc=oc, dst_=dst_, bias=bias: v.tensor_scalar(out=dst_.f[:, oc * TW:(oc + 1) * TW], in0=self.pb[5][:, 0:TW], scalar1=cc[bias].f[:, oc:oc + 1], scalar2=None, op0=ALU.add), r=[self.pbt[5], cc[bias].tok], w=[dst_.tok], dbg=f"p4{nm}e{ti}_{oc}")
            softplus(qf.f, qf.tok, DC * TW, f"p4sq{ti}")
            softplus(kf.f, kf.tok, DC * TW, f"p4sk{ti}")
            S.dma("sdma", qf2.fr(0, 64, 0, DC * TW), qf.fr(64, 128, 0, DC * TW),
                  r=[qf.tok], w=[qf2.tok], dbg=f"p4q2{ti}")
            S.dma("sdma", kf2.fr(0, 64, 0, DC * TW), kf.fr(64, 128, 0, DC * TW),
                  r=[kf.tok], w=[kf2.tok], dbg=f"p4k2{ti}")
            for tc in range(2):
                for nm, wres in (("v", wv_), ("k2", wk)):
                    for nhalf in range(2):
                        for kc in range(DC):
                            S.op("pe", lambda t, tc=tc, nhalf=nhalf, kc=kc, wres=wres, nm=nm: t.matmul(
                                self.pb[6][:, 0:512],
                                xhb.b[:, kc * TW + tc * P:kc * TW + (tc + 1) * P],
                                wres.b[:, kc * D + nhalf * 512:kc * D + (nhalf + 1) * 512],
                                start=(kc == 0), stop=(kc == DC - 1 and nm == "k2")),
                                r=[xhb.tok, wres.tok], w=[self.pbt[6]], dbg=f"p4t{nm}{ti}_{tc}_{nhalf}")
                        if nm == "v":
                            S.op("pe", lambda t, nhalf=nhalf: t.matmul(self.pb[6][:, 0:512], cc["ones_row"].fr(0, 1, 0, P), bvr.fr(0, 1, nhalf * 512, (nhalf + 1) * 512), start=False, stop=True), r=[cc["ones_row"].tok, bvr.tok], w=[self.pbt[6]], dbg=f"p4bv{ti}_{tc}_{nhalf}")
                            S.op("dve", lambda v, tc=tc, nhalf=nhalf: v.tensor_copy(
                                out=vo.f[:, 0:2 * NH * 66].rearrange("p (c h e) -> p c h e", c=2, h=NH)[:, tc, nhalf * 8:(nhalf + 1) * 8, 0:64],
                                in_=self.pb[6][:, 0:512].rearrange("p (h e) -> p h e", h=8)),
                                r=[self.pbt[6]], w=[vo.tok], dbg=f"p4ve{ti}_{tc}_{nhalf}")
                        else:
                            S.op("pe", lambda t, nhalf=nhalf: t.matmul(self.pb[6][:, 0:512], cc["ones_row"].fr(0, 1, 0, P), bkr.fr(0, 1, nhalf * 512, (nhalf + 1) * 512), start=False, stop=True), r=[cc["ones_row"].tok, bkr.tok], w=[self.pbt[6]], dbg=f"p4bk{ti}_{tc}_{nhalf}")
                            S.op("dve", lambda v, tc=tc, nhalf=nhalf: v.tensor_copy(out=ktok.f[:, tc * D + nhalf * 512:tc * D + (nhalf + 1) * 512], in_=self.pb[6][:, 0:512]), r=[self.pbt[6]], w=[ktok.tok], dbg=f"p4ke{ti}_{tc}_{nhalf}")
            softplus(ktok.f, ktok.tok, 2 * D, f"p4s2{ti}")
            SW = 66
            for tc in range(2):
                for grp in range(4):
                    for hh in range(4):
                        h = grp * 4 + hh
                        ksrc = kf if h % 2 == 0 else kf2
                        qsrc = qf if h % 2 == 0 else qf2
                        chs = h // 2
                        S.op("pe", lambda t, hh=hh, chs=chs, tc=tc, ksrc=ksrc, qsrc=qsrc: t.matmul(
                            self.pb[0][:, hh * P:(hh + 1) * P],
                            ksrc.fr(0, 64, chs * TW + tc * P, chs * TW + (tc + 1) * P),
                            qsrc.fr(0, 64, chs * TW + tc * P, chs * TW + (tc + 1) * P),
                            start=True, stop=True), r=[ksrc.tok, qsrc.tok], w=[self.pbt[0]], dbg=f"p4S{ti}_{tc}_{h}")
                    S.op("dve", lambda v: v.tensor_tensor(out=sT.f, in0=self.pb[0][:, 0:4 * P], in1=cc["mask4"].f, op=ALU.mult), r=[self.pbt[0], cc["mask4"].tok], w=[sT.tok], dbg=f"p4m{ti}_{tc}_{grp}")
                    for hh in range(4):
                        h = grp * 4 + hh
                        qsrc = qf if h % 2 == 0 else qf2
                        chs = h // 2
                        S.op("pe", lambda t, h=h, hh=hh, tc=tc: t.matmul(
                            self.pb[1][:, hh * SW:hh * SW + SW], sT.f[:, hh * P:(hh + 1) * P],
                            vo.f[:, (tc * NH + h) * SW:(tc * NH + h + 1) * SW],
                            start=True, stop=False), r=[sT.tok, vo.tok], w=[self.pbt[1]], dbg=f"p4n{ti}_{tc}_{h}")
                        S.op("pe", lambda t, h=h, hh=hh, chs=chs, tc=tc, qsrc=qsrc: t.matmul(
                            self.pb[1][:, hh * SW:hh * SW + SW],
                            qsrc.fr(0, 64, chs * TW + tc * P, chs * TW + (tc + 1) * P),
                            Mst.fr(0, 64, h * SW, (h + 1) * SW),
                            start=False, stop=True),
                            r=[qsrc.tok, Mst.tok], w=[self.pbt[1]], dbg=f"p4i{ti}_{tc}_{h}")
                        S.op("pe", lambda t, h=h, hh=hh, tc=tc: t.matmul(
                            self.pb[4][0:64, hh * SW:hh * SW + SW],
                            ktok.f[:, tc * D + h * 64:tc * D + (h + 1) * 64],
                            vo.f[:, (tc * NH + h) * SW:(tc * NH + h + 1) * SW],
                            start=True, stop=True),
                            r=[ktok.tok, vo.tok], w=[self.pbt[4]], dbg=f"p4u{ti}_{tc}_{h}")
                    for hh in range(4):
                        h = grp * 4 + hh
                        S.op("dve", lambda v, hh=hh: v.reciprocal(out=dcol.f[:, 0:1], in_=self.pb[1][:, hh * SW + 64:hh * SW + 65]), r=[self.pbt[1]], w=[dcol.tok], dbg=f"p4r{ti}_{tc}_{h}")
                        S.op("dve", lambda v, h=h, hh=hh, tc=tc: v.tensor_scalar(out=att.b[:, tc * D + h * 64:tc * D + (h + 1) * 64], in0=self.pb[1][:, hh * SW:hh * SW + 64], scalar1=dcol.f[:, 0:1], scalar2=None, op0=ALU.mult), r=[self.pbt[1], dcol.tok], w=[att.tok], dbg=f"p4d{ti}_{tc}_{h}")
                        S.op("dve", lambda v, h=h, hh=hh: v.tensor_tensor(out=Mst.fr(0, 64, h * SW, (h + 1) * SW), in0=Mst.fr(0, 64, h * SW, (h + 1) * SW), in1=self.pb[4][0:64, hh * SW:hh * SW + SW], op=ALU.add), r=[Mst.tok, self.pbt[4]], w=[Mst.tok], dbg=f"p4M{ti}_{tc}_{h}")
            if self.dbg_name == "att" and ti == 0:
                S.dma("sdma", self.dbg_att, att.b[:, 0:2 * D], r=[att.tok], w=[self.htok["y"]], dbg="attdmp")
            for tc in range(2):
                for oc in range(DC):
                    S.op("pe", lambda t, tc=tc, oc=oc: t.transpose(self.pbh[:, 0:P], att.b[:, tc * D + oc * P:tc * D + (oc + 1) * P], cc["identb"].b[:, 0:P]), r=[att.tok, cc["identb"].tok], w=[self.pbht], dbg=f"p4t{ti}_{tc}_{oc}")
                    S.op("dve", lambda v, tc=tc, oc=oc: v.tensor_copy(out=atf.b[:, oc * TW + tc * P:oc * TW + (tc + 1) * P], in_=self.pbh[:, 0:P]), r=[self.pbht], w=[atf.tok], dbg=f"p4tc{ti}_{tc}_{oc}")
            for oc in range(DC):
                for kc in range(DC):
                    S.op("pe", lambda t, oc=oc, kc=kc: t.matmul(self.pb[5][:, 0:TW], wo.b[:, kc * D + oc * P:kc * D + (oc + 1) * P], atf.b[:, kc * TW:(kc + 1) * TW], start=(kc == 0), stop=(kc == DC - 1)), r=[atf.tok, wo.tok], w=[self.pbt[5]], dbg=f"p4o{ti}_{oc}")
                S.op("dve", lambda v, oc=oc, x_=x_: v.scalar_tensor_tensor(out=yt.f[:, oc * TW:(oc + 1) * TW], in0=self.pb[5][:, 0:TW], scalar=cc["bo"].f[:, oc:oc + 1], in1=x_.f[:, oc * TW:(oc + 1) * TW], op0=ALU.add, op1=ALU.add), r=[self.pbt[5], cc["bo"].tok, x_.tok], w=[yt.tok], dbg=f"p4y{ti}_{oc}")
            S.dma("adma", hb["x3"].rearrange("(c p) t -> p c t", p=P)[:, :, t0:t0 + TW],
                  yt.f.rearrange("p (c t) -> p c t", c=DC), r=[yt.tok],
                  w=[self.stok["x3"][ti]], dbg=f"p4yo{ti}")
        A.free(wq, wk, wv_, wo, bvr, bkr, xt[0], xhb, sqr, sqf, qf, kf, qf2, kf2,
               ktok, vo, sT, att, atf, dcol, Mst, yt)

    def finish(self):
        S, hb = self.S, self.hb
        if self.stop is not None:
            srcmap = {"p1": "xmid", "p2": "x1", "p3": "x2", "p4": "x3"}
            nm = srcmap[self.stop]
            S.dma("sdma", hb["y"], hb[nm], r=list(self.stok[nm]), w=[self.htok["y"]], dbg="ycp")
        S.op("dve", lambda v: v.nop(), r=[self.htok["y"]], dbg="final")


def _run_built(pr, in_maps, core_ids, **kw):
    pr.S.assign()
    with ExitStack() as st:
        sems = {}
        for s in Sched.SEMS:
            sems[s] = st.enter_context(pr.nc.semaphore("sem_" + s))
        with pr.nc.Block() as block:
            @block.tensor
            def _(tensor):
                pr.S.run_engine("pe", tensor, sems)

            @block.vector
            def _(vector):
                pr.S.run_engine("dve", vector, sems)

            @block.scalar
            def _(scalar):
                pr.S.run_engine("act", scalar, sems)

            @block.gpsimd
            def _(gpsimd):
                pr.S.run_engine("pool", gpsimd, sems)

            @block.sync
            def _(sync):
                pr.S.run_engine("sync", sync, sems)
        return run_bass_kernel_spmd(pr.nc, in_maps, core_ids=core_ids, **kw)


def kernel(**inputs):
    pr = Prog(_DEV_STOP, _DEV_DBG)
    with pr.stack:
        pr.build()
        shared = _prep_shared(inputs)
        x = np.asarray(inputs["L0_attn_a_x"], np.float32)
        in_maps = []
        for core in range(8):
            m = dict(shared)
            m["x_fm"] = np.ascontiguousarray(x[core].T)
            in_maps.append(m)
        res = _run_built(pr, in_maps, list(range(8)))
    out = np.empty((8, T, D), np.float32)
    for core in range(8):
        out[core] = res.results[core]["y"].T
    return out
